# version 08
# baseline (speedup 1.0000x reference)
import numpy as np

import concourse.bass as bass
import concourse.mybir as mybir
from concourse.bass_utils import run_bass_kernel_spmd

F32 = mybir.dt.float32
BF16 = mybir.dt.bfloat16
U8 = mybir.dt.uint8
AF = mybir.ActivationFunctionType
ALU = mybir.AluOpType
AX = mybir.AxisListType

B, T, C = 2, 8192, 1024
NCORE = 8
TOK = T * B // NCORE
NT_OWN = TOK // 128
NT_BATCH = T // 128
KC = C // 128
NL = 12
NH, HD = 16, 64
NE, HF = 8, 2048
HC = HF // 128
SCALE = 0.125
LN_EPS = 1e-5
TB = float(T)

SEM_NAMES = ["ld0", "lda", "ldb", "ldc", "wl", "w2a", "w2b", "wa0", "wa1",
             "st0", "st1", "dg", "pe", "dv", "ac"]


class Ctx:

    def __init__(self, nc, cur, eng, sems):
        self.nc = nc
        self.cur = cur
        self.eng = eng
        self.sems = sems
        self.n = dict.fromkeys(SEM_NAMES, 0)

    def inc(self, instr, sem, amount):
        self.n[sem] += amount
        if instr is not None:
            instr.then_inc(self.sems[sem], amount)
        return instr

    def wait(self, engname, sem, value=None):
        if self.cur == engname:
            self.eng.wait_ge(self.sems[sem], self.n[sem] if value is None else value)

    def mm(self, out, lhsT, rhs, start=True, stop=True, tp=None, sem=None):
        ins = None
        if self.cur == "tensor":
            ins = self.nc.tensor.matmul(out, lhsT, rhs, start=start, stop=stop,
                                        tile_position=tp)
        if sem is not None:
            self.inc(ins, sem, 1)
        return ins

    def tr(self, out, in_, ident, tp=None, sem=None):
        ins = None
        if self.cur == "tensor":
            ins = self.nc.tensor.transpose(out, in_, ident, tile_position=tp)
        if sem is not None:
            self.inc(ins, sem, 1)
        return ins

    def dma(self, q, out, in_, sem=None, **kw):
        ins = None
        if self.cur == q:
            ins = getattr(self.nc, q).dma_start(out=out, in_=in_, **kw)
        if sem is not None:
            self.inc(ins, sem, 16)
        return ins

    def v(self, fn, *a, sem=None, **kw):
        ins = None
        if self.cur == "vector":
            self.eng.wait_ge(self.sems["dv"], self.n["dv"])
            ins = getattr(self.nc.vector, fn)(*a, **kw)
        self.inc(ins, "dv", 1)
        return ins

    def s(self, fn, *a, sem=None, **kw):
        ins = None
        if self.cur == "scalar":
            self.eng.wait_ge(self.sems["ac"], self.n["ac"])
            ins = getattr(self.nc.scalar, fn)(*a, **kw)
        self.inc(ins, "ac", 1)
        return ins


def _carve(nc, arena, plan):
    base = nc.lookup_mloc(arena).addr
    out = {}
    for name, (off, shape, dt) in plan.items():
        h = nc.virtual_sbuf_tensor(name, list(shape), dt)
        nc._tensor(h.name, list(h.shape), dt, type="SB")
        mloc = nc.lookup_mloc(h)
        mloc.allocated = True
        mloc.addr = base + off
        out[name] = h
    return out


def build_nc(dbg=(), nlayers=NL, nexperts=NE, phases=4, p2_until=None):
    nc = bass.Bass("TRN2", target_bir_lowering=False)
    KB = 1024

    x_batch = nc.dram_tensor("x_batch", [T, C], F32, kind="ExternalInput")
    eye32_d = nc.dram_tensor("eye32", [128, 128], F32, kind="ExternalInput")
    eyebf_d = nc.dram_tensor("eyebf", [128, 128], BF16, kind="ExternalInput")
    wq_d = nc.dram_tensor("wq", [NL, C, C], F32, kind="ExternalInput")
    wkT_d = nc.dram_tensor("wkT", [NL, C, C], F32, kind="ExternalInput")
    wvT_d = nc.dram_tensor("wvT", [NL, C, C], F32, kind="ExternalInput")
    wpT_d = nc.dram_tensor("wpT", [NL, C, C], F32, kind="ExternalInput")
    bqkv_d = nc.dram_tensor("bqkv", [NL, 3 * C], F32, kind="ExternalInput")
    bproj_d = nc.dram_tensor("bproj", [NL, C], F32, kind="ExternalInput")
    gwT_d = nc.dram_tensor("gwT", [C, NE], F32, kind="ExternalInput")
    gb_d = nc.dram_tensor("gb", [1, NE], F32, kind="ExternalInput")
    w0T_d = nc.dram_tensor("w0T", [NE, C, HF], F32, kind="ExternalInput")
    w1T_d = nc.dram_tensor("w1T", [NE, C, HF], F32, kind="ExternalInput")
    w2T_d = nc.dram_tensor("w2T", [NE, HF, C], F32, kind="ExternalInput")
    b0_d = nc.dram_tensor("b0", [NE, HF], F32, kind="ExternalInput")
    b1_d = nc.dram_tensor("b1", [NE, HF], F32, kind="ExternalInput")
    b2_d = nc.dram_tensor("b2", [NE, C], F32, kind="ExternalInput")
    out_d = nc.dram_tensor("out", [TOK, C], F32, kind="ExternalOutput")
    xnT_sp = nc.dram_tensor("xnT_sp", [KC, 128, TOK], F32)
    sT_sp = nc.dram_tensor("sT_sp", [KC, 128, TOK], F32)
    acc_d = nc.dram_tensor("acc_d", [NT_OWN, 128, C], F32)

    dbg_d = {}
    for name in dbg:
        shape = {"G": [C, C], "m": [C, 1], "E": [C, C], "c": [C, 1],
                 "wgt": [TOK, NE], "sT": [KC * 128, TOK],
                 "xnT": [KC * 128, TOK]}[name]
        dbg_d[name] = nc.dram_tensor("dbg_" + name, shape, F32,
                                     kind="ExternalOutput")

    A = nc.alloc_sbuf_tensor
    eye32 = A("sb_eye32", [128, 128], F32)
    eyebf = A("sb_eyebf", [128, 128], BF16)
    onesbf = A("sb_onesbf", [1, 128], BF16)
    ones32 = A("sb_ones32", [1, 128], F32)
    onescol = A("sb_onescol", [128, 1], BF16)
    gwT = A("sb_gwT", [128, KC * NE], F32)
    gb = A("sb_gb", [1, NE], F32)
    wgt = A("sb_wgt", [128, NT_OWN * NE], F32)
    lgall = A("sb_lgall", [128, NT_OWN * NE], F32)
    lgsc = A("sb_lgsc", [128, 4 * NT_OWN], F32)
    lg2 = A("sb_lg2", [128, NT_OWN * NE], F32)
    Gbf = A("sb_Gbf", [128, KC * C], BF16)
    mcol = A("sb_mcol", [128, KC], F32)
    mbf = A("sb_mbf", [128, KC], BF16)
    ccol = A("sb_ccol", [128, KC], F32)
    cbf = A("sb_cbf", [128, KC], BF16)
    rcol = A("sb_rcol", [128, KC], F32)
    rbf = A("sb_rbf", [128, KC], BF16)
    u1bf = A("sb_u1bf", [128, KC], BF16)
    Bbf = A("sb_Bbf", [128, C], BF16)
    Btbf = A("sb_Btbf", [128, C], BF16)
    bqcol = A("sb_bqcol", [128, KC], BF16)
    bpcol = A("sb_bpcol", [128, KC], F32)
    bkrow = A("sb_bkrow", [1, C], BF16)
    bvrow = A("sb_bvrow", [1, C], BF16)
    bvrow32 = A("sb_bvrow32", [1, C], F32)
    urow = A("sb_urow", [1, C], F32)
    urowbf = A("sb_urowbf", [1, C], BF16)
    rkrow = A("sb_rkrow", [1, C], BF16)
    lnb = A("sb_ln", [128, 16], F32)
    sh1bf = A("sb_sh1bf", [128, 512], BF16)
    b0col = A("sb_b0col", [128, 2 * HC], F32)
    b1col = A("sb_b1col", [128, 2 * HC], F32)
    b2rbf = A("sb_b2rbf", [1, 2 * C], BF16)
    t32a = A("sb_t32a", [128, 1024], F32)
    t32b = A("sb_t32b", [128, 1024], F32)
    t32c = A("sb_t32c", [128, 1024], F32)
    t32d = A("sb_t32d", [128, 1024], F32)

    ARENA = 144 * KB
    arena = A("arena", [128, ARENA], U8)
    sv = _carve(nc, arena, {
        "xnbf":  (0, [128, NT_BATCH * C], BF16),
        "E":     (0 * KB, [128, KC * C], F32),
        "Ebf":   (32 * KB, [128, KC * C], BF16),
        "wkTbf": (48 * KB, [128, KC * C], BF16),
        "wvTbf": (64 * KB, [128, KC * C], BF16),
        "wpTbf": (80 * KB, [128, KC * C], BF16),
        "wqbf":  (96 * KB, [128, KC * C], BF16),
        "bigA":  (112 * KB, [128, KC * C], BF16),
        "bigB":  (128 * KB, [128, KC * C], BF16),
        "xnTbf3": (48 * KB, [128, KC * TOK], BF16),
        "xnTbf": (0 * KB, [128, KC * TOK], BF16),
        "w0bf":  (32 * KB, [128, KC * HF], BF16),
        "w1bf":  (64 * KB, [128, KC * HF], BF16),
        "w2bf4": (96 * KB, [128, HC * C], BF16),
        "actbf": (128 * KB, [128, HC * 512], BF16),
    })
    xnbf = sv["xnbf"]
    E, Ebf = sv["E"], sv["Ebf"]
    wkTbf, wvTbf, wpTbf, wqbf = sv["wkTbf"], sv["wvTbf"], sv["wpTbf"], sv["wqbf"]
    bigA, bigB = sv["bigA"], sv["bigB"]
    xnTbf3, xnTbf = sv["xnTbf3"], sv["xnTbf"]
    w0bf, w1bf, w2bf4, actbf = sv["w0bf"], sv["w1bf"], sv["w2bf4"], sv["actbf"]

    psum = nc.alloc_psum_tensor("psum", [128, 4096], F32)

    sems = {s: nc.alloc_semaphore(name="sem_" + s) for s in SEM_NAMES}

    def emit(c: Ctx):
        mk = {}

        def mark(name):
            mk[name] = dict(c.n)

        def wmk(engname, name, sem):
            if c.cur == engname:
                c.eng.wait_ge(c.sems[sem], mk[name][sem])

        guard = {b: [] for b in range(8)}

        def bank(b, cols=512):
            return psum[:, b * 512:b * 512 + cols]

        def bank_bf(b):
            return psum[:, b * 512:b * 512 + 64].bitcast(BF16)

        def bg_wait(b):
            if c.cur == "tensor":
                for sem, cnt in guard[b]:
                    c.eng.wait_ge(c.sems[sem], cnt)
            guard[b] = []

        def bg_set(b, sem):
            guard[b].append((sem, c.n[sem]))

        c.dma("sync", eye32[:], eye32_d.ap(), sem="ld0")
        c.dma("sync", eyebf[:], eyebf_d.ap(), sem="ld0")
        c.dma("sync", gwT[:].rearrange("p (c e) -> p c e", e=NE),
              gwT_d.ap().rearrange("(c p) e -> p c e", p=128), sem="ld0")
        c.dma("sync", gb[:], gb_d.ap(), sem="ld0")
        p0_ld = c.n["ld0"]
        c.v("memset", onesbf[:], 1.0)
        c.v("memset", ones32[:], 1.0)
        c.v("memset", onescol[:], 1.0, sem="dv")
        mark("p0")

        xv = [t32a, t32b]
        acsq = {}
        ldi = {}
        for i in range(NT_BATCH + 1):
            if i < NT_BATCH:
                if i >= 2:
                    c.wait("sync", "dv", mk[f"xn_{i - 2}"]["dv"])
                ldp = "lda" if i % 2 == 0 else "ldb"
                c.dma("sync", xv[i % 2][:],
                      x_batch.ap()[i * 128:(i + 1) * 128, :], sem=ldp)
                ldi[i] = c.n[ldp]
            if i >= 1:
                j = i - 1
                ownj = j < NT_OWN
                b8 = (j % 2) * 8
                xj = xv[j % 2]
                c.wait("vector", "ac", acsq[j])
                c.v("tensor_scalar", lnb[:, b8 + 3:b8 + 4], lnb[:, b8 + 2:b8 + 3],
                    1.0 / C, LN_EPS, op0=ALU.mult, op1=ALU.add)
                c.v("tensor_tensor", lnb[:, b8 + 4:b8 + 5], lnb[:, b8 + 1:b8 + 2],
                    lnb[:, b8 + 1:b8 + 2], op=ALU.mult)
                c.v("tensor_sub", lnb[:, b8 + 5:b8 + 6], lnb[:, b8 + 3:b8 + 4],
                    lnb[:, b8 + 4:b8 + 5], sem="dv")
                c.wait("scalar", "dv", c.n["dv"])
                if j >= 2:
                    c.wait("scalar", "dv", mk[f"xn_{j - 2}"]["dv"])
                c.s("activation", lnb[:, b8 + 6:b8 + 7], lnb[:, b8 + 5:b8 + 6],
                    AF.Sqrt, sem="ac")
                c.wait("vector", "ac", c.n["ac"])
                c.v("reciprocal", lnb[:, b8 + 7:b8 + 8], lnb[:, b8 + 6:b8 + 7])
                if 1 <= j <= NT_OWN:
                    wmk("vector", f"tr_{j - 1}", "pe")
                if ownj:
                    c.v("tensor_scalar", t32c[:], xj[:], lnb[:, b8 + 1:b8 + 2],
                        lnb[:, b8 + 7:b8 + 8],
                        op0=ALU.subtract, op1=ALU.mult, sem="dv")
                    mark(f"xn_{j}")
                    c.wait("scalar", "dv", c.n["dv"])
                    c.s("copy", xnbf[:, j * C:(j + 1) * C], t32c[:], sem="ac")
                else:
                    c.v("tensor_scalar", xnbf[:, j * C:(j + 1) * C], xj[:],
                        lnb[:, b8 + 1:b8 + 2], lnb[:, b8 + 7:b8 + 8],
                        op0=ALU.subtract, op1=ALU.mult, sem="dv")
                    mark(f"xn_{j}")
                if j >= NT_OWN:
                    c.wait("tensor", "dv", mk[f"xn_{j}"]["dv"])
                    if j == NT_OWN:
                        for b in range(8):
                            bg_wait(b)
                    for ic in range(KC):
                        c.mm(bank(ic),
                             xnbf[:, j * C + ic * 128: j * C + (ic + 1) * 128],
                             xnbf[:, j * C: j * C + 512],
                             start=(j == NT_OWN), stop=False)
                if ownj:
                    c.wait("tensor", "ld0", p0_ld)
                    c.wait("tensor", "dv", mk[f"xn_{j}"]["dv"])
                    for g in range(2):
                        b = 4 + g
                        bg_wait(b)
                        for f4 in range(4):
                            f = g * 4 + f4
                            c.tr(psum[:, b * 512 + f4 * 128:b * 512 + f4 * 128 + 128],
                                 t32c[:, f * 128:(f + 1) * 128], eye32[:, :],
                                 sem="pe" if f4 == 3 else None)
                        c.wait("vector", "pe", c.n["pe"])
                        c.v("tensor_copy", t32d[:, g * 512:(g + 1) * 512],
                            bank(b), sem="dv")
                        bg_set(b, "dv")
                    mark(f"tr_{j}")
                    c.wait("tensor", "dv", c.n["dv"])
                    bg_wait(3)
                    for f in range(KC):
                        c.mm(psum[:, 3 * 512:3 * 512 + NE],
                             t32d[:, f * 128:(f + 1) * 128],
                             gwT[:, f * NE:(f + 1) * NE],
                             start=(f == 0), stop=False)
                    c.mm(psum[:, 3 * 512:3 * 512 + NE], ones32[:], gb[:],
                         start=False, stop=True, sem="pe")
                    mark(f"gate_{j}")
                    c.wait("scalar", "pe", c.n["pe"])
                    c.s("copy", lgall[:, j * NE:(j + 1) * NE],
                        psum[:, 3 * 512:3 * 512 + NE], sem="ac")
                    bg_set(3, "ac")
                    c.wait("sync", "dv", c.n["dv"])
                    for f in range(KC):
                        c.dma("sync", xnT_sp.ap()[f, :, j * 128:(j + 1) * 128],
                              t32d[:, f * 128:(f + 1) * 128], sem="st0")
                    mark(f"spill_{j}")
            if i < NT_BATCH:
                b8i = (i % 2) * 8
                ldp = "lda" if i % 2 == 0 else "ldb"
                c.wait("scalar", ldp, ldi[i])
                if 1 <= i <= NT_OWN:
                    wmk("scalar", f"gate_{i - 1}", "pe")
                    wmk("scalar", f"spill_{i - 1}", "st0")
                if i >= 2:
                    c.wait("scalar", "dv", mk[f"xn_{i - 2}"]["dv"])
                c.s("activation", t32d[:], xv[i % 2][:], AF.Square,
                    accum_out=lnb[:, b8i + 2:b8i + 3], sem="ac")
                acsq[i] = c.n["ac"]
                c.wait("vector", ldp, ldi[i])
                c.v("reduce_sum", lnb[:, b8i + 0:b8i + 1], xv[i % 2][:],
                    axis=AX.X)
                c.v("tensor_scalar_mul", lnb[:, b8i + 1:b8i + 2],
                    lnb[:, b8i + 0:b8i + 1], 1.0 / C)
        mark("p1_tiles")

        LG3 = lgall[:].rearrange("p (t e) -> p t e", e=NE)
        W3 = wgt[:].rearrange("p (t e) -> p t e", e=NE)
        L23 = lg2[:].rearrange("p (t e) -> p t e", e=NE)
        m1c = lgsc[:, 0 * NT_OWN:1 * NT_OWN]
        m2c = lgsc[:, 1 * NT_OWN:2 * NT_OWN]
        d21c = lgsc[:, 2 * NT_OWN:3 * NT_OWN]
        invc = lgsc[:, 3 * NT_OWN:4 * NT_OWN]
        c.wait("vector", "ac", c.n["ac"])
        c.v("tensor_reduce", m1c.unsqueeze(2), LG3, axis=AX.X, op=ALU.max)
        c.v("scalar_tensor_tensor", L23, m1c.unsqueeze(2).broadcast_to(
            (128, NT_OWN, NE)), -1.0, LG3, op0=ALU.mult, op1=ALU.add)
        c.v("tensor_scalar", L23, L23, 0.0, -1e30, op0=ALU.is_ge, op1=ALU.mult)
        c.v("tensor_add", L23, L23, LG3)
        c.v("tensor_reduce", m2c.unsqueeze(2), L23, axis=AX.X, op=ALU.max,
            sem="dv")
        c.wait("vector", "dv", c.n["dv"])
        c.v("tensor_sub", d21c, m2c, m1c, sem="dv")
        c.wait("scalar", "dv", c.n["dv"])
        c.s("activation", d21c, d21c, AF.Exp, sem="ac")
        c.wait("vector", "ac", c.n["ac"])
        c.v("tensor_scalar_add", invc, d21c, 1.0)
        c.v("reciprocal", invc, invc)
        c.v("tensor_tensor", L23, L23, m2c.unsqueeze(2).broadcast_to(
            (128, NT_OWN, NE)), op=ALU.is_ge)
        c.v("tensor_tensor", L23, L23, d21c.unsqueeze(2).broadcast_to(
            (128, NT_OWN, NE)), op=ALU.mult)
        c.v("tensor_tensor", W3, LG3, m1c.unsqueeze(2).broadcast_to(
            (128, NT_OWN, NE)), op=ALU.is_ge)
        c.v("tensor_add", W3, W3, L23)
        c.v("tensor_tensor", W3, W3, invc.unsqueeze(2).broadcast_to(
            (128, NT_OWN, NE)), op=ALU.mult, sem="dv")
        mark("wgt_done")

        c.wait("tensor", "ac", mk["p1_tiles"]["ac"])
        for t in range(NT_OWN):
            for ic in range(KC):
                c.mm(bank(ic),
                     xnbf[:, t * C + ic * 128: t * C + (ic + 1) * 128],
                     xnbf[:, t * C: t * C + 512],
                     start=False, stop=(t == NT_OWN - 1),
                     sem="pe" if t == NT_OWN - 1 else None)
        c.wait("scalar", "pe", c.n["pe"])
        for ic in range(KC):
            c.s("copy", Gbf[:, ic * C: ic * C + 512], bank(ic), sem="ac")
        for b in range(8):
            bg_set(b, "ac")
        for b in range(8):
            bg_wait(b)
        for t in range(NT_BATCH):
            for ic in range(KC):
                c.mm(bank(ic),
                     xnbf[:, t * C + ic * 128: t * C + (ic + 1) * 128],
                     xnbf[:, t * C + 512: t * C + 1024],
                     start=(t == 0), stop=(t == NT_BATCH - 1),
                     sem="pe" if t == NT_BATCH - 1 else None)
        c.wait("scalar", "pe", c.n["pe"])
        for ic in range(KC):
            c.s("copy", Gbf[:, ic * C + 512: ic * C + 1024], bank(ic), sem="ac")
        for b in range(8):
            bg_set(b, "ac")
        mark("G")
        c.wait("tensor", "dv", mk["p0"]["dv"])
        for ic in range(KC):
            bg_wait(0)
            for t in range(NT_BATCH):
                c.mm(psum[:, 0:1],
                     xnbf[:, t * C + ic * 128: t * C + (ic + 1) * 128],
                     onescol[:],
                     start=(t == 0), stop=(t == NT_BATCH - 1),
                     sem="pe" if t == NT_BATCH - 1 else None)
            c.wait("vector", "pe", c.n["pe"])
            c.v("tensor_copy", mcol[:, ic:ic + 1], psum[:, 0:1], sem="dv")
            bg_set(0, "dv")
        c.wait("scalar", "dv", c.n["dv"])
        c.s("copy", mbf[:], mcol[:], sem="ac")
        mark("p1_done")

        if "G" in dbg_d:
            c.wait("gpsimd", "ac", mk["G"]["ac"])
            c.dma("gpsimd", dbg_d["G"].ap().rearrange("(c p) n -> p c n", p=128),
                  Gbf[:].rearrange("p (c n) -> p c n", n=C), sem="dg")
        if "m" in dbg_d:
            c.wait("gpsimd", "dv", mk["p1_done"]["dv"])
            c.dma("gpsimd", dbg_d["m"].ap().rearrange("(c p) one -> p c one", p=128),
                  mcol[:].unsqueeze(2), sem="dg", allow_slow_non_contiguous=True)
        if "wgt" in dbg_d:
            c.wait("gpsimd", "dv", mk["p1_done"]["dv"])
            c.dma("gpsimd", dbg_d["wgt"].ap().rearrange("(t p) e -> p t e", p=128),
                  wgt[:].rearrange("p (t e) -> p t e", e=NE), sem="dg")
        if "xnT" in dbg_d:
            c.wait("gpsimd", "st0", mk["p1_done"]["st0"])
            c.dma("gpsimd", dbg_d["xnT"].ap().rearrange("(c p) t -> c p t", p=128),
                  xnT_sp.ap(), sem="dg")

        if phases < 2:
            c.wait("sync", "st0", c.n["st0"])
            if dbg_d:
                c.wait("sync", "dg", c.n["dg"])
            return
        c.wait("vector", "pe", mk["p1_done"]["pe"])
        c.v("memset", E[:], 0.0)
        c.v("memset", ccol[:], 0.0, sem="dv")
        mark("p2_init")

        wlc = {}
        ldcc = {}

        def load_weight(name, dst, srcd, li2):
            c.wait("gpsimd", "wl", c.n["wl"])
            c.dma("gpsimd", dst[:].rearrange("p (cc n) -> p cc n", n=C),
                  srcd.ap()[li2].rearrange("(cc p) n -> p cc n", p=128),
                  sem="wl")
            wlc[(name, li2)] = c.n["wl"]

        def load_bq(li2):
            c.wait("gpsimd", "wl", c.n["wl"])
            for f in range(KC):
                c.dma("gpsimd", bqcol[:, f:f + 1],
                      bqkv_d.ap()[li2:li2 + 1, f * 128:(f + 1) * 128]
                      .rearrange("one p -> p one"), sem="wl")
            wlc[("bq", li2)] = c.n["wl"]

        def load_bkv(li2):
            c.wait("gpsimd", "wl", c.n["wl"])
            c.dma("gpsimd", bkrow[0:1, 0:C],
                  bqkv_d.ap()[li2:li2 + 1, C:2 * C], sem="wl")
            c.dma("gpsimd", bvrow[0:1, 0:C],
                  bqkv_d.ap()[li2:li2 + 1, 2 * C:3 * C], sem="wl")
            wlc[("bkv", li2)] = c.n["wl"]

        def load_biases_g(li2):
            load_bq(li2)
            load_bkv(li2)

        def load_bv32(li2):
            c.wait("sync", "ldc", c.n["ldc"])
            c.dma("sync", bvrow32[0:1, 0:C],
                  bqkv_d.ap()[li2:li2 + 1, 2 * C:3 * C], sem="ldc")
            wlc[("bv32", li2)] = c.n["ldc"]

        def load_bp(li2):
            c.wait("sync", "ldc", c.n["ldc"])
            for f in range(KC):
                c.dma("sync", bpcol[:, f:f + 1],
                      bproj_d.ap()[li2:li2 + 1, f * 128:(f + 1) * 128]
                      .rearrange("one p -> p one"), sem="ldc")
            wlc[("bp", li2)] = c.n["ldc"]

        def load_biases_s(li2):
            load_bv32(li2)
            load_bp(li2)
            ldcc[li2] = c.n["ldc"]

        for li in range(nlayers):
            s_ = 0
            sr = 0
            if li == 0:
                c.wait("gpsimd", "pe", mk["p1_done"]["pe"])
                c.wait("gpsimd", "dv", mk["p2_init"]["dv"])
                for name, dst, srcd in (("wk", wkTbf, wkT_d), ("wv", wvTbf, wvT_d),
                                        ("wp", wpTbf, wpT_d), ("wq", wqbf, wq_d)):
                    load_weight(name, dst, srcd, 0)
                load_biases_g(0)
                load_biases_s(0)
            wl_l = None

            c.wait("scalar", "dv", c.n["dv"])
            c.s("copy", cbf[:], ccol[:], sem="ac")
            mark(f"L{li}_snap")
            if p2_until == "snap":
                break

            if li == 0:
                c.v("tensor_copy", rcol[:], mcol[:], sem="dv")
            else:
                c.wait("tensor", "ac", c.n["ac"])
                bg_wait(5)
                for jc in range(KC):
                    for ic in range(KC):
                        c.mm(psum[:, 5 * 512 + jc:5 * 512 + jc + 1],
                             Ebf[:, ic * C + jc * 128: ic * C + (jc + 1) * 128],
                             mbf[:, ic:ic + 1],
                             start=(ic == 0), stop=(ic == KC - 1),
                             sem="pe" if ic == KC - 1 else None)
                c.wait("vector", "pe", c.n["pe"])
                c.v("tensor_add", rcol[:], psum[:, 5 * 512:5 * 512 + KC], mcol[:])
                c.v("scalar_tensor_tensor", rcol[:], ccol[:], TB, rcol[:],
                    op0=ALU.mult, op1=ALU.add, sem="dv")
                bg_set(5, "dv")
            c.wait("scalar", "dv", c.n["dv"])
            c.s("copy", rbf[:], rcol[:], sem="ac")
            ac_rbf = c.n["ac"]

            c.wait("tensor", "wl", max(wlc[(nm, li)] for nm in ("wk", "wv", "wp", "wq", "bq", "bkv")))
            c.wait("tensor", "ac", ac_rbf)
            bg_wait(6)
            bg_wait(7)
            for nh_ in range(2):
                for ic in range(KC):
                    c.mm(psum[0:1, (6 + nh_) * 512:(6 + nh_) * 512 + 512],
                         rbf[:, ic:ic + 1],
                         wkTbf[:, ic * C + nh_ * 512: ic * C + nh_ * 512 + 512],
                         start=(ic == 0), stop=(ic == KC - 1),
                         sem="pe" if ic == KC - 1 else None)
            c.wait("scalar", "pe", c.n["pe"])
            c.s("copy", rkrow[:], psum[0:1, 6 * 512:8 * 512], sem="ac")
            bg_set(6, "ac")
            bg_set(7, "ac")
            bg_wait(6)
            bg_wait(7)
            for nh_ in range(2):
                for ic in range(KC):
                    c.mm(psum[0:1, (6 + nh_) * 512:(6 + nh_) * 512 + 512],
                         rbf[:, ic:ic + 1],
                         wvTbf[:, ic * C + nh_ * 512: ic * C + nh_ * 512 + 512],
                         start=(ic == 0), stop=(ic == KC - 1),
                         sem="pe" if ic == KC - 1 else None)
            c.wait("vector", "pe", c.n["pe"])
            c.wait("vector", "ldc", wlc[("bv32", li)])
            c.v("scalar_tensor_tensor", urow[:], bvrow32[0:1, sr:sr + C], TB,
                psum[0:1, 6 * 512:8 * 512], op0=ALU.mult, op1=ALU.add, sem="dv")
            bg_set(6, "dv")
            bg_set(7, "dv")
            c.wait("scalar", "dv", c.n["dv"])
            c.s("copy", urowbf[:], urow[:], sem="ac")
            mark(f"L{li}_rows")
            if p2_until == "rows":
                break

            def stream(lhsT_at, rhs_at, evac, tag):
                pp = [0]
                for oc in range(KC):
                    for jh in range(2):
                        b = pp[0]
                        bg_wait(b)
                        for kc in range(KC):
                            c.mm(bank(b), lhsT_at(kc, oc), rhs_at(kc, jh),
                                 start=(kc == 0), stop=(kc == KC - 1),
                                 sem="pe" if kc == KC - 1 else None)
                        evac(oc, jh, b)
                        pp[0] = (pp[0] + 1) % 4
                mark(tag)

            def ev_m1(oc, jh, b):
                c.wait("vector", "pe", c.n["pe"])
                c.v("tensor_tensor",
                    bigA[:, oc * C + jh * 512: oc * C + jh * 512 + 512],
                    bank(b), Gbf[:, oc * C + jh * 512: oc * C + jh * 512 + 512],
                    op=ALU.add, sem="dv")
                bg_set(b, "dv")

            if li == 0:
                c.wait("vector", "ac", mk["p1_done"]["ac"])
                c.v("tensor_copy", bigB[:], Gbf[:], sem="dv")
            else:
                stream(lambda kc, oc: Gbf[:, kc * C + oc * 128: kc * C + (oc + 1) * 128],
                       lambda kc, jh: Ebf[:, kc * C + jh * 512: kc * C + jh * 512 + 512],
                       ev_m1, f"L{li}_m1")
            if p2_until == "m1":
                break

            def ev_m2(oc, jh, b):
                c.wait("vector", "pe", c.n["pe"])
                c.v("tensor_tensor",
                    bigB[:, oc * C + jh * 512: oc * C + jh * 512 + 512],
                    bank(b), bigA[:, oc * C + jh * 512: oc * C + jh * 512 + 512],
                    op=ALU.add, sem="dv")
                bg_set(b, "dv")

            if li > 0:
                c.wait("tensor", "dv", c.n["dv"])
                stream(lambda kc, oc: Ebf[:, kc * C + oc * 128: kc * C + (oc + 1) * 128],
                       lambda kc, jh: bigA[:, kc * C + jh * 512: kc * C + jh * 512 + 512],
                       ev_m2, f"L{li}_m2")
            if p2_until == "m2":
                break

            def ev_m3(oc, jh, b):
                c.wait("scalar", "pe", c.n["pe"])
                c.s("copy", bigA[:, oc * C + jh * 512: oc * C + jh * 512 + 512],
                    bank(b), sem="ac")
                bg_set(b, "ac")

            c.wait("tensor", "dv", c.n["dv"])
            stream(lambda kc, jc: bigB[:, kc * C + jc * 128: kc * C + (jc + 1) * 128],
                   lambda kc, oh: wkTbf[:, kc * C + oh * 512: kc * C + oh * 512 + 512],
                   ev_m3, f"L{li}_m3")
            if li + 1 < nlayers:
                wmk("gpsimd", f"L{li}_m3", "pe")
                load_weight("wk", wkTbf, wkT_d, li + 1)
            if p2_until == "m3":
                break

            c.wait("tensor", "ac", c.n["ac"])
            bg_wait(2)
            bg_wait(3)
            for h in range(NH):
                pair, r0 = h // 2, (h % 2) * 64
                o0 = pair * 128 + r0
                oap = psum[r0:r0 + 64, 1024 + o0:1024 + o0 + 64]
                for jc in range(KC):
                    c.mm(oap, bigA[:, jc * C + h * 64: jc * C + h * 64 + 64],
                         wvTbf[:, jc * C + h * 64: jc * C + h * 64 + 64],
                         start=(jc == 0), stop=False, tp=(0, r0))
                c.mm(oap, rkrow[0:1, h * 64:h * 64 + 64],
                     bvrow[0:1, sr + h * 64:sr + h * 64 + 64],
                     start=False, stop=False, tp=(0, r0))
                c.mm(oap, bkrow[0:1, sr + h * 64:sr + h * 64 + 64],
                     urowbf[0:1, h * 64:h * 64 + 64], start=False, stop=True,
                     tp=(0, r0), sem="pe")
            c.wait("scalar", "pe", c.n["pe"])
            c.s("activation", Bbf[:], psum[:, 1024:2048], AF.Identity,
                scale=SCALE, sem="ac")
            bg_set(2, "ac")
            bg_set(3, "ac")
            mark(f"L{li}_B")
            if li + 1 < nlayers:
                wmk("gpsimd", f"L{li}_B", "pe")
                load_weight("wv", wvTbf, wvT_d, li + 1)
                load_bkv(li + 1)
            elif p2_until is None and phases >= 3:
                wmk("gpsimd", f"L{li}_B", "pe")
                c.wait("gpsimd", "st0", mk["p1_done"]["st0"])
                c.wait("gpsimd", "wl", c.n["wl"])
                c.dma("gpsimd",
                      xnTbf3[:].rearrange("p (cc t) -> p cc t", t=TOK),
                      xnT_sp.ap().rearrange("cc p t -> p cc t"), sem="wl")
                wlc[("xnT3", 0)] = c.n["wl"]
            if p2_until == "B":
                break

            c.wait("tensor", "ac", c.n["ac"])
            bg_wait(4)
            for f in range(KC):
                c.tr(psum[:, 4 * 512 + f * 64:4 * 512 + f * 64 + 64].bitcast(BF16),
                     Bbf[:, f * 128:(f + 1) * 128], eyebf[:, :],
                     sem="pe" if f == KC - 1 else None)
            c.wait("vector", "pe", c.n["pe"])
            c.v("tensor_copy", Btbf[:], bank(4).bitcast(BF16), sem="dv")
            bg_set(4, "dv")
            mark(f"L{li}_Bt")
            if p2_until == "Bt":
                break

            bg_wait(2)
            for h in range(NH):
                pair, r0 = h // 2, (h % 2) * 64
                c.mm(psum[r0:r0 + 64, 2 * 512 + pair:2 * 512 + pair + 1],
                     Bbf[r0:r0 + 64, pair * 128 + r0:pair * 128 + r0 + 64],
                     bqcol[r0:r0 + 64, pair:pair + 1],
                     start=True, stop=True, tp=(r0, r0),
                     sem="pe" if h == NH - 1 else None)
            c.wait("scalar", "pe", c.n["pe"])
            c.s("copy", u1bf[:], psum[:, 2 * 512:2 * 512 + KC], sem="ac")
            bg_set(2, "ac")
            mark(f"L{li}_u1")
            if li + 1 < nlayers:
                wmk("gpsimd", f"L{li}_u1", "pe")
                load_bq(li + 1)
                c.wait("sync", "dv", mk[f"L{li}_rows"]["dv"])
                load_bv32(li + 1)
            if p2_until == "u1":
                break

            c.wait("tensor", "dv", c.n["dv"])
            for pair in range(KC):
                b0_, b1_ = 0, 1
                bg_wait(b0_)
                bg_wait(b1_)
                for half in range(2):
                    bb = b0_ if half == 0 else b1_
                    for sub in range(2):
                        r0 = sub * 64
                        c.mm(psum[r0:r0 + 64, bb * 512:bb * 512 + 512],
                             Btbf[r0:r0 + 64,
                                  pair * 128 + r0:pair * 128 + r0 + 64],
                             wpTbf[r0:r0 + 64,
                                   pair * C + half * 512:pair * C + half * 512 + 512],
                             start=True, stop=True, tp=(r0, r0),
                             sem="pe" if sub == 1 else None)
                c.wait("scalar", "pe", c.n["pe"])
                c.s("copy", bigB[:, pair * C:pair * C + 512], bank(b0_), sem="ac")
                c.s("copy", bigB[:, pair * C + 512:(pair + 1) * C], bank(b1_),
                    sem="ac")
                bg_set(b0_, "ac")
                bg_set(b1_, "ac")
            mark(f"L{li}_m5")
            if p2_until == "m5":
                break

            def ev_m6(ic, jh, b):
                c.wait("scalar", "pe", c.n["pe"])
                c.s("copy", bigA[:, ic * C + jh * 512: ic * C + jh * 512 + 512],
                    bank(b), sem="ac")
                c.wait("vector", "ac", c.n["ac"])
                c.v("tensor_tensor",
                    E[:, ic * C + jh * 512: ic * C + jh * 512 + 512],
                    E[:, ic * C + jh * 512: ic * C + jh * 512 + 512],
                    bank(b), op=ALU.add, sem="dv")
                bg_set(b, "dv")

            c.wait("tensor", "ac", c.n["ac"])
            stream(lambda oc, ic: wqbf[:, oc * C + ic * 128: oc * C + (ic + 1) * 128],
                   lambda oc, jh: bigB[:, oc * C + jh * 512: oc * C + jh * 512 + 512],
                   ev_m6, f"L{li}_m6")
            if li + 1 < nlayers:
                wmk("gpsimd", f"L{li}_m6", "pe")
                load_weight("wq", wqbf, wq_d, li + 1)
            if p2_until == "m6":
                break

            c.wait("tensor", "ac", c.n["ac"])
            bg_wait(2)
            if li > 0:
                for jc in range(KC):
                    for kc in range(KC):
                        c.mm(psum[:, 2 * 512 + 8 + jc:2 * 512 + 8 + jc + 1],
                             bigA[:, kc * C + jc * 128: kc * C + (jc + 1) * 128],
                             cbf[:, kc:kc + 1],
                             start=(kc == 0), stop=(kc == KC - 1),
                             sem="pe" if kc == KC - 1 else None)
            for jc in range(KC):
                for yc in range(KC):
                    c.mm(psum[:, 2 * 512 + 16 + jc:2 * 512 + 16 + jc + 1],
                         wpTbf[:, yc * C + jc * 128: yc * C + (jc + 1) * 128],
                         u1bf[:, yc:yc + 1],
                         start=(yc == 0), stop=(yc == KC - 1),
                         sem="pe" if yc == KC - 1 else None)
            c.wait("vector", "pe", c.n["pe"])
            c.wait("vector", "ldc", wlc[("bp", li)])
            if li > 0:
                c.v("tensor_add", ccol[:], ccol[:], psum[:, 2 * 512 + 8:2 * 512 + 16])
            c.v("tensor_add", ccol[:], ccol[:], psum[:, 2 * 512 + 16:2 * 512 + 24])
            c.v("tensor_add", ccol[:], ccol[:], bpcol[:, s_:s_ + KC], sem="dv")
            bg_set(2, "dv")
            mark(f"L{li}_c")
            if li + 1 < nlayers:
                wmk("gpsimd", f"L{li}_c", "pe")
                load_weight("wp", wpTbf, wpT_d, li + 1)
                c.wait("sync", "dv", mk[f"L{li}_c"]["dv"])
                load_bp(li + 1)
            if p2_until == "c":
                break

            if li == 0:
                c.wait("scalar", "dv", c.n["dv"])
                for ic in range(KC):
                    c.s("copy", Ebf[:, ic * C:(ic + 1) * C],
                        E[:, ic * C:(ic + 1) * C], sem="ac")
                mark(f"L{li}_done")
                continue
            for kc in range(KC):
                b = 4 + (kc % 2)
                bg_wait(b)
                for ic in range(KC):
                    c.tr(psum[:, b * 512 + ic * 64:b * 512 + ic * 64 + 64]
                         .bitcast(BF16),
                         Ebf[:, ic * C + kc * 128: ic * C + (kc + 1) * 128],
                         eyebf[:, :], sem="pe" if ic == KC - 1 else None)
                c.wait("vector", "pe", c.n["pe"])
                c.v("tensor_copy", bigB[:, kc * C:(kc + 1) * C],
                    bank(b).bitcast(BF16), sem="dv")
                bg_set(b, "dv")
            mark(f"L{li}_Et")
            if p2_until == "Et":
                break

            def ev_m7(ic, jh, b):
                c.wait("vector", "pe", c.n["pe"])
                c.v("tensor_tensor",
                    E[:, ic * C + jh * 512: ic * C + jh * 512 + 512],
                    E[:, ic * C + jh * 512: ic * C + jh * 512 + 512],
                    bank(b), op=ALU.add, sem="dv")
                bg_set(b, "dv")
                if jh == 1:
                    c.wait("scalar", "dv", c.n["dv"])
                    c.s("copy", Ebf[:, ic * C:(ic + 1) * C],
                        E[:, ic * C:(ic + 1) * C], sem="ac")

            c.wait("tensor", "dv", c.n["dv"])
            c.wait("tensor", "ac", c.n["ac"])
            stream(lambda kc, ic: bigB[:, kc * C + ic * 128: kc * C + (ic + 1) * 128],
                   lambda kc, jh: bigA[:, kc * C + jh * 512: kc * C + jh * 512 + 512],
                   ev_m7, f"L{li}_m7")
            mark(f"L{li}_done")

        if "E" in dbg_d:
            c.wait("gpsimd", "dv", c.n["dv"])
            c.wait("gpsimd", "wl", c.n["wl"])
            c.dma("gpsimd", dbg_d["E"].ap().rearrange("(cc p) n -> p cc n", p=128),
                  E[:].rearrange("p (cc n) -> p cc n", n=C), sem="dg")
        if "c" in dbg_d:
            c.wait("gpsimd", "dv", c.n["dv"])
            c.wait("gpsimd", "dg", c.n["dg"])
            c.dma("gpsimd", dbg_d["c"].ap().rearrange("(cc p) one -> p cc one", p=128),
                  ccol[:].unsqueeze(2), sem="dg", allow_slow_non_contiguous=True)

        if phases < 3:
            c.wait("sync", "st0", c.n["st0"])
            if dbg_d:
                c.wait("sync", "dg", c.n["dg"])
            return
        if ("xnT3", 0) not in wlc:
            c.wait("gpsimd", "pe", c.n["pe"])
            c.wait("gpsimd", "st0", mk["p1_done"]["st0"])
            c.wait("gpsimd", "wl", c.n["wl"])
            c.dma("gpsimd", xnTbf3[:].rearrange("p (cc t) -> p cc t", t=TOK),
                  xnT_sp.ap().rearrange("cc p t -> p cc t"), sem="wl")
            wlc[("xnT3", 0)] = c.n["wl"]
        wl_p3 = wlc[("xnT3", 0)]
        if nlayers == 0:
            c.wait("scalar", "dv", c.n["dv"])
            c.s("copy", Ebf[:], E[:], sem="ac")
        mark("p3_ebf")
        wlc4 = {}
        ldc4 = {}

        def load_w01(e2):
            c.wait("gpsimd", "wl", c.n["wl"])
            c.dma("gpsimd", w0bf[:].rearrange("p (cc n) -> p cc n", n=HF),
                  w0T_d.ap()[e2].rearrange("(cc p) n -> p cc n", p=128),
                  sem="wl")
            wlc4[("w0", e2)] = c.n["wl"]
            c.wait("gpsimd", "wl", c.n["wl"])
            c.dma("gpsimd", w1bf[:].rearrange("p (cc n) -> p cc n", n=HF),
                  w1T_d.ap()[e2].rearrange("(cc p) n -> p cc n", p=128),
                  sem="wl")
            c.dma("gpsimd", b2rbf[0:1, (e2 % 2) * C:(e2 % 2) * C + C],
                  b2_d.ap()[e2:e2 + 1, :], sem="wl")
            wlc4[("w01", e2)] = c.n["wl"]

        def load_w2e(e2):
            c.wait("gpsimd", "wl", c.n["wl"])
            c.dma("gpsimd", w2bf4[:].rearrange("p (cc n) -> p cc n", n=C),
                  w2T_d.ap()[e2].rearrange("(cc p) n -> p cc n", p=128),
                  sem="wl")
            wlc4[("w2", e2)] = c.n["wl"]

        def load_b01(e2):
            s4 = (e2 % 2) * HC
            c.wait("sync", "ldc", c.n["ldc"])
            for hcc in range(HC):
                c.dma("sync", b0col[:, s4 + hcc:s4 + hcc + 1],
                      b0_d.ap()[e2:e2 + 1, hcc * 128:(hcc + 1) * 128]
                      .rearrange("one p -> p one"), sem="ldc")
                c.dma("sync", b1col[:, s4 + hcc:s4 + hcc + 1],
                      b1_d.ap()[e2:e2 + 1, hcc * 128:(hcc + 1) * 128]
                      .rearrange("one p -> p one"), sem="ldc")
            ldc4[e2] = c.n["ldc"]

        if phases >= 4:
            c.wait("gpsimd", "ac", mk["p3_ebf"]["ac"])
            if "E" in dbg_d or "c" in dbg_d:
                c.wait("gpsimd", "dg", c.n["dg"])
            c.wait("gpsimd", "wl", c.n["wl"])
            c.dma("gpsimd", xnTbf[:].rearrange("p (cc t) -> p cc t", t=TOK),
                  xnT_sp.ap().rearrange("cc p t -> p cc t"), sem="wl")
            wl_p4 = c.n["wl"]
            load_w2e(0)
            load_b01(0)
        for tc in range(TOK // 512):
            for oc in range(KC):
                it = tc * KC + oc
                s4_ = it % 4
                xt = [t32a, t32b][s4_ // 2]
                xh = (s4_ % 2) * 512
                ldp = ("lda", "ldb", "w2a", "w2b")[s4_]
                stp = "st0" if it % 2 == 0 else "st1"
                et = [t32c, t32d][s4_ // 2]
                if it >= 4:
                    c.wait("sync", "dv", mk[f"p3_ev_{it - 4}"]["dv"])
                else:
                    c.wait("sync", "dv", mk["p1_tiles"]["dv"])
                c.dma("sync", xt[:, xh:xh + 512],
                      xnT_sp.ap()[oc, :, tc * 512:(tc + 1) * 512], sem=ldp)
                ld_it = c.n[ldp]
                b = s4_
                bg_wait(b)
                c.wait("tensor", "wl", wl_p3)
                c.wait("tensor", "ac", mk["p3_ebf"]["ac"])
                for ic in range(KC):
                    c.mm(bank(b),
                         Ebf[:, ic * C + oc * 128: ic * C + (oc + 1) * 128],
                         xnTbf3[:, ic * TOK + tc * 512: ic * TOK + tc * 512 + 512],
                         start=(ic == 0), stop=(ic == KC - 1),
                         sem="pe" if ic == KC - 1 else None)
                c.wait("vector", "pe", c.n["pe"])
                c.wait("vector", ldp, ld_it)
                if it >= 2:
                    c.wait("vector", stp, mk[f"p3_st_{it - 2}"][stp])
                c.v("scalar_tensor_tensor", et[:, xh:xh + 512], bank(b),
                    ccol[:, oc:oc + 1], xt[:, xh:xh + 512],
                    op0=ALU.add, op1=ALU.add, sem="dv")
                bg_set(b, "dv")
                mark(f"p3_ev_{it}")
                c.wait("sync", "dv", c.n["dv"])
                c.dma("sync", sT_sp.ap()[oc, :, tc * 512:(tc + 1) * 512],
                      et[:, xh:xh + 512], sem=stp)
                mark(f"p3_st_{it}")
        mark("p3_done")
        if "sT" in dbg_d:
            c.wait("gpsimd", "st0", c.n["st0"])
            c.wait("gpsimd", "st1", c.n["st1"])
            c.wait("gpsimd", "wl", c.n["wl"])
            c.dma("gpsimd", dbg_d["sT"].ap().rearrange("(cc p) t -> cc p t", p=128),
                  sT_sp.ap(), sem="dg")

        if phases < 4:
            c.wait("sync", "st0", c.n["st0"])
            c.wait("sync", "st1", c.n["st1"])
            if dbg_d:
                c.wait("sync", "dg", c.n["dg"])
            return
        c.wait("gpsimd", "pe", mk["p3_done"]["pe"])

        for e in range(nexperts):
            s4 = (e % 2) * HC
            sb2 = (e % 2) * C
            if e == 0:
                load_w01(0)
            else:
                wmk("gpsimd", f"p4_mm_{e - 1}", "pe")
                load_w2e(e)
            wl_e = wlc4[("w01", e)]
            ld_e = ldc4[e]

            for tq in range(4):
                t0 = tq * 512
                c.wait("tensor", "wl", wlc4[("w0", e)])
                c.wait("tensor", "wl", wl_p4)
                for hcc in range(HC):
                    b0_ = hcc % 2
                    b1_ = 2 + hcc % 2
                    bg_wait(b0_)
                    for dc in range(KC):
                        c.mm(bank(b0_),
                             w0bf[:, dc * HF + hcc * 128: dc * HF + (hcc + 1) * 128],
                             xnTbf[:, dc * TOK + t0: dc * TOK + t0 + 512],
                             start=(dc == 0), stop=(dc == KC - 1),
                             sem="pe" if dc == KC - 1 else None)
                    pe_h0 = c.n["pe"]
                    if hcc == 0:
                        c.wait("tensor", "wl", wl_e)
                    bg_wait(b1_)
                    for dc in range(KC):
                        c.mm(bank(b1_),
                             w1bf[:, dc * HF + hcc * 128: dc * HF + (hcc + 1) * 128],
                             xnTbf[:, dc * TOK + t0: dc * TOK + t0 + 512],
                             start=(dc == 0), stop=(dc == KC - 1),
                             sem="pe" if dc == KC - 1 else None)
                    c.wait("scalar", "pe", pe_h0)
                    c.wait("scalar", "ldc", ld_e)
                    c.s("activation", actbf[:, hcc * 512:(hcc + 1) * 512],
                        bank(b0_), AF.Identity, bias=b0col[:, s4 + hcc:s4 + hcc + 1],
                        sem="ac")
                    bg_set(b0_, "ac")
                    c.wait("scalar", "pe", c.n["pe"])
                    c.wait("scalar", "dv", c.n["dv"])
                    c.s("activation", sh1bf[:], bank(b1_), AF.Silu,
                        bias=b1col[:, s4 + hcc:s4 + hcc + 1], sem="ac")
                    bg_set(b1_, "ac")
                    c.wait("vector", "ac", c.n["ac"])
                    c.v("tensor_mul", actbf[:, hcc * 512:(hcc + 1) * 512],
                        actbf[:, hcc * 512:(hcc + 1) * 512], sh1bf[:], sem="dv")
                mark(f"p4_h_{e}_{tq}")
                if tq == 3 and e + 1 < nexperts:
                    wmk("gpsimd", f"p4_h_{e}_3", "pe")
                    load_w01(e + 1)
                    wmk("sync", f"p4_h_{e}_3", "ac")
                    load_b01(e + 1)

                for b in range(8):
                    bg_wait(b)
                c.wait("tensor", "dv", c.n["dv"])
                c.wait("tensor", "wl", wlc4[("w2", e)])
                for hcc in range(HC):
                    for mt in range(4):
                        for dch in range(2):
                            c.mm(bank(mt * 2 + dch),
                                 actbf[:, hcc * 512 + mt * 128:
                                       hcc * 512 + (mt + 1) * 128],
                                 w2bf4[:, hcc * C + dch * 512:
                                       hcc * C + dch * 512 + 512],
                                 start=(hcc == 0), stop=False,
                                 sem="pe" if (mt == 3 and dch == 1) else None)
                for mt in range(4):
                    for dch in range(2):
                        c.mm(bank(mt * 2 + dch), onesbf[:],
                             b2rbf[0:1, sb2 + dch * 512:sb2 + dch * 512 + 512],
                             start=False, stop=True, sem="pe")
                for mt in range(4):
                    tglob = tq * 4 + mt
                    dst_t = t32a if mt % 2 == 0 else t32b
                    c.wait("vector", "pe", c.n["pe"])
                    wcol = wgt[:, tglob * NE + e: tglob * NE + e + 1]
                    wap = "wa0" if mt % 2 == 0 else "wa1"
                    if e == 0 and tq == 0 and mt < 2:
                        for ss in ("lda", "ldb", "w2a", "w2b"):
                            c.wait("vector", ss, mk["p3_done"][ss])
                    if mt >= 2 or e > 0 or tq > 0:
                        c.wait("vector", wap, c.n[wap])
                    for dch in range(2):
                        c.v("tensor_scalar_mul",
                            dst_t[:, dch * 512:dch * 512 + 512],
                            bank(mt * 2 + dch), wcol, sem="dv")
                        bg_set(mt * 2 + dch, "dv")
                    c.wait("gpsimd", "dv", c.n["dv"])
                    c.wait("gpsimd", wap, c.n[wap])
                    if e > 0:
                        c.dma("gpsimd", acc_d.ap()[tglob], dst_t[:],
                              sem=wap, accum_op=ALU.add)
                    else:
                        c.dma("gpsimd", acc_d.ap()[tglob], dst_t[:], sem=wap)
                    mark(f"p4_acc_{e}_{tq}_{mt}")
            mark(f"p4_mm_{e}")
            mark(f"p4_act_{e}")

        c.wait("sync", "st0", c.n["st0"])
        c.wait("sync", "st1", c.n["st1"])
        for tglob in range(NT_OWN):
            mtp = "wa0" if (tglob % 4) % 2 == 0 else "wa1"
            accmark = f"p4_acc_{nexperts - 1}_{tglob // 4}_{tglob % 4}"
            wmk("sync", accmark, mtp)
            if tglob >= 1:
                wmk("sync", f"p4_fin_{tglob - 1}", "dv")
            c.dma("sync", t32c[:], acc_d.ap()[tglob], sem="ldb")
            ldacc = c.n["ldb"]
            c.wait("scalar", "ldb", ldacc)
            if tglob == 0:
                c.wait("scalar", "st0", mk["p3_done"]["st0"])
                c.wait("scalar", "st1", mk["p3_done"]["st1"])
            c.s("activation", t32c[:], t32c[:], AF.Silu, sem="ac")
            mark(f"p4_g_{tglob}")
            if tglob >= 1:
                wmk("sync", f"p4_str_{tglob - 1}", "pe")
                wmk("sync", f"p4_out_{tglob - 1}", "st0")
            for oc in range(KC):
                c.dma("sync", t32d[:, oc * 128:(oc + 1) * 128],
                      sT_sp.ap()[oc, :, tglob * 128:(tglob + 1) * 128],
                      sem="lda")
            ld_s = c.n["lda"]
            c.wait("tensor", "lda", ld_s)
            for g in range(2):
                b = 4 + g
                bg_wait(b)
                for f4 in range(4):
                    oc = g * 4 + f4
                    c.tr(psum[:, b * 512 + f4 * 128:b * 512 + f4 * 128 + 128],
                         t32d[:, oc * 128:(oc + 1) * 128], eye32[:, :],
                         sem="pe" if f4 == 3 else None)
                c.wait("vector", "pe", c.n["pe"])
                if g == 0 and tglob == 0:
                    c.wait("vector", "st0", mk["p3_done"]["st0"])
                    c.wait("vector", "st1", mk["p3_done"]["st1"])
                if g == 0 and tglob >= 1:
                    wmk("vector", f"p4_out_{tglob - 1}", "st0")
                c.v("tensor_copy", t32d[:, g * 512:(g + 1) * 512],
                    bank(b), sem="dv")
                bg_set(b, "dv")
            mark(f"p4_str_{tglob}")
            c.wait("vector", "ac", mk[f"p4_g_{tglob}"]["ac"])
            c.v("tensor_mul", t32d[:], t32d[:], t32c[:], sem="dv")
            mark(f"p4_mul_{tglob}")
            c.wait("sync", "dv", mk[f"p4_mul_{tglob}"]["dv"])
            c.dma("sync", t32c[:],
                  x_batch.ap()[tglob * 128:(tglob + 1) * 128, :], sem="ldb")
            ld_x = c.n["ldb"]
            c.wait("vector", "ldb", ld_x)
            c.v("tensor_add", t32d[:], t32d[:], t32c[:], sem="dv")
            mark(f"p4_fin_{tglob}")
            c.wait("sync", "dv", c.n["dv"])
            c.dma("sync", out_d.ap()[tglob * 128:(tglob + 1) * 128, :],
                  t32d[:], sem="st0")
            mark(f"p4_out_{tglob}")
        c.wait("sync", "st0", c.n["st0"])
        c.wait("gpsimd", "st0", c.n["st0"])
        if dbg_d:
            c.wait("sync", "dg", c.n["dg"])

    with nc.Block() as block:
        @block.sync
        def _(eng):
            emit(Ctx(nc, "sync", eng, sems))

        @block.gpsimd
        def _(eng):
            emit(Ctx(nc, "gpsimd", eng, sems))

        @block.tensor
        def _(eng):
            emit(Ctx(nc, "tensor", eng, sems))

        @block.vector
        def _(eng):
            emit(Ctx(nc, "vector", eng, sems))

        @block.scalar
        def _(eng):
            emit(Ctx(nc, "scalar", eng, sems))

    return nc


def make_inputs(inputs):
    import ml_dtypes
    x = np.ascontiguousarray(np.asarray(inputs["L0_attn_inputs_embeds"],
                                        np.float32))
    Wqkv = np.asarray(inputs["L0_attn_Wqkv"], np.float32)
    com = {
        "eye32": np.eye(128, dtype=np.float32),
        "eyebf": np.eye(128, dtype=np.float32).astype(ml_dtypes.bfloat16),
        "wq": np.ascontiguousarray(Wqkv[:, 0:C, :]),
        "wkT": np.ascontiguousarray(Wqkv[:, C:2 * C, :].transpose(0, 2, 1)),
        "wvT": np.ascontiguousarray(Wqkv[:, 2 * C:3 * C, :].transpose(0, 2, 1)),
        "wpT": np.ascontiguousarray(
            np.asarray(inputs["L0_attn_Wproj"], np.float32).transpose(0, 2, 1)),
        "bqkv": np.ascontiguousarray(np.asarray(inputs["L0_attn_bqkv"],
                                                np.float32)),
        "bproj": np.ascontiguousarray(np.asarray(inputs["L0_attn_bproj"],
                                                 np.float32)),
        "gwT": np.ascontiguousarray(np.asarray(inputs["L0_ffn_gate_w"],
                                               np.float32).T),
        "gb": np.asarray(inputs["L0_ffn_gate_b"], np.float32).reshape(1, NE),
        "w0T": np.ascontiguousarray(np.asarray(inputs["L0_ffn_w0"],
                                               np.float32).transpose(0, 2, 1)),
        "w1T": np.ascontiguousarray(np.asarray(inputs["L0_ffn_w1"],
                                               np.float32).transpose(0, 2, 1)),
        "w2T": np.ascontiguousarray(np.asarray(inputs["L0_ffn_w2"],
                                               np.float32).transpose(0, 2, 1)),
        "b0": np.ascontiguousarray(np.asarray(inputs["L0_ffn_b0"], np.float32)),
        "b1": np.ascontiguousarray(np.asarray(inputs["L0_ffn_b1"], np.float32)),
        "b2": np.ascontiguousarray(np.asarray(inputs["L0_ffn_b2"], np.float32)),
    }
    in_maps = []
    for core in range(NCORE):
        bi, si = core // 4, core % 4
        xb = x[bi]
        own = xb[si * TOK:(si + 1) * TOK]
        rest = np.concatenate([xb[:si * TOK], xb[(si + 1) * TOK:]], axis=0)
        m = dict(com)
        m["x_batch"] = np.ascontiguousarray(np.concatenate([own, rest], axis=0))
        in_maps.append(m)
    return in_maps


def kernel(**inputs):
    in_maps = make_inputs(inputs)
    nc = build_nc()
    res = run_bass_kernel_spmd(nc, in_maps, core_ids=list(range(NCORE)))
    out = np.empty((B, T, C), np.float32)
    for core in range(NCORE):
        bi, si = core // 4, core % 4
        out[bi, si * TOK:(si + 1) * TOK, :] = res.results[core]["out"]
    return out


# version 09
# speedup vs baseline: 1.0277x; 1.0277x over previous
import numpy as np

import concourse.bass as bass
import concourse.mybir as mybir
from concourse.bass_utils import run_bass_kernel_spmd

F32 = mybir.dt.float32
BF16 = mybir.dt.bfloat16
U8 = mybir.dt.uint8
AF = mybir.ActivationFunctionType
ALU = mybir.AluOpType
AX = mybir.AxisListType

B, T, C = 2, 8192, 1024
NCORE = 8
TOK = T * B // NCORE
NT_OWN = TOK // 128
NT_BATCH = T // 128
KC = C // 128
NL = 12
NH, HD = 16, 64
NE, HF = 8, 2048
HC = HF // 128
SCALE = 0.125
LN_EPS = 1e-5
TB = float(T)

SEM_NAMES = ["ld0", "lda", "ldb", "ldc", "wl", "w2a", "w2b", "wa0", "wa1",
             "st0", "st1", "dg", "pe", "dv", "ac"]


class Ctx:

    def __init__(self, nc, cur, eng, sems):
        self.nc = nc
        self.cur = cur
        self.eng = eng
        self.sems = sems
        self.n = dict.fromkeys(SEM_NAMES, 0)

    def inc(self, instr, sem, amount):
        self.n[sem] += amount
        if instr is not None:
            instr.then_inc(self.sems[sem], amount)
        return instr

    def wait(self, engname, sem, value=None):
        if self.cur == engname:
            self.eng.wait_ge(self.sems[sem], self.n[sem] if value is None else value)

    def mm(self, out, lhsT, rhs, start=True, stop=True, tp=None, sem=None):
        ins = None
        if self.cur == "tensor":
            ins = self.nc.tensor.matmul(out, lhsT, rhs, start=start, stop=stop,
                                        tile_position=tp)
        if sem is not None:
            self.inc(ins, sem, 1)
        return ins

    def tr(self, out, in_, ident, tp=None, sem=None):
        ins = None
        if self.cur == "tensor":
            ins = self.nc.tensor.transpose(out, in_, ident, tile_position=tp)
        if sem is not None:
            self.inc(ins, sem, 1)
        return ins

    def dma(self, q, out, in_, sem=None, **kw):
        ins = None
        if self.cur == q:
            ins = getattr(self.nc, q).dma_start(out=out, in_=in_, **kw)
        if sem is not None:
            self.inc(ins, sem, 16)
        return ins

    def v(self, fn, *a, sem=None, **kw):
        ins = None
        if self.cur == "vector":
            self.eng.wait_ge(self.sems["dv"], self.n["dv"])
            ins = getattr(self.nc.vector, fn)(*a, **kw)
        self.inc(ins, "dv", 1)
        return ins

    def s(self, fn, *a, sem=None, **kw):
        ins = None
        if self.cur == "scalar":
            self.eng.wait_ge(self.sems["ac"], self.n["ac"])
            ins = getattr(self.nc.scalar, fn)(*a, **kw)
        self.inc(ins, "ac", 1)
        return ins


def _carve(nc, arena, plan):
    base = nc.lookup_mloc(arena).addr
    out = {}
    for name, (off, shape, dt) in plan.items():
        h = nc.virtual_sbuf_tensor(name, list(shape), dt)
        nc._tensor(h.name, list(h.shape), dt, type="SB")
        mloc = nc.lookup_mloc(h)
        mloc.allocated = True
        mloc.addr = base + off
        out[name] = h
    return out


def build_nc(dbg=(), nlayers=NL, nexperts=NE, phases=4, p2_until=None):
    nc = bass.Bass("TRN2", target_bir_lowering=False)
    KB = 1024

    x_batch = nc.dram_tensor("x_batch", [T, C], F32, kind="ExternalInput")
    eye32_d = nc.dram_tensor("eye32", [128, 128], F32, kind="ExternalInput")
    eyebf_d = nc.dram_tensor("eyebf", [128, 128], BF16, kind="ExternalInput")
    wq_d = nc.dram_tensor("wq", [NL, C, C], F32, kind="ExternalInput")
    wkT_d = nc.dram_tensor("wkT", [NL, C, C], F32, kind="ExternalInput")
    wvT_d = nc.dram_tensor("wvT", [NL, C, C], F32, kind="ExternalInput")
    wpT_d = nc.dram_tensor("wpT", [NL, C, C], F32, kind="ExternalInput")
    bqkv_d = nc.dram_tensor("bqkv", [NL, 3 * C], F32, kind="ExternalInput")
    bproj_d = nc.dram_tensor("bproj", [NL, C], F32, kind="ExternalInput")
    gwT_d = nc.dram_tensor("gwT", [C, NE], F32, kind="ExternalInput")
    gb_d = nc.dram_tensor("gb", [1, NE], F32, kind="ExternalInput")
    w0T_d = nc.dram_tensor("w0T", [NE, C, HF], F32, kind="ExternalInput")
    w1T_d = nc.dram_tensor("w1T", [NE, C, HF], F32, kind="ExternalInput")
    w2T_d = nc.dram_tensor("w2T", [NE, HF, C], F32, kind="ExternalInput")
    b0_d = nc.dram_tensor("b0", [NE, HF], F32, kind="ExternalInput")
    b1_d = nc.dram_tensor("b1", [NE, HF], F32, kind="ExternalInput")
    b2_d = nc.dram_tensor("b2", [NE, C], F32, kind="ExternalInput")
    out_d = nc.dram_tensor("out", [TOK, C], F32, kind="ExternalOutput")
    xnT_sp = nc.dram_tensor("xnT_sp", [KC, 128, TOK], F32)
    sT_sp = nc.dram_tensor("sT_sp", [KC, 128, TOK], F32)
    acc_d = nc.dram_tensor("acc_d", [NT_OWN, 128, C], F32)

    dbg_d = {}
    for name in dbg:
        shape = {"G": [C, C], "m": [C, 1], "E": [C, C], "c": [C, 1],
                 "wgt": [TOK, NE], "sT": [KC * 128, TOK],
                 "xnT": [KC * 128, TOK]}[name]
        dbg_d[name] = nc.dram_tensor("dbg_" + name, shape, F32,
                                     kind="ExternalOutput")

    A = nc.alloc_sbuf_tensor
    eye32 = A("sb_eye32", [128, 128], F32)
    eyebf = A("sb_eyebf", [128, 128], BF16)
    onesbf = A("sb_onesbf", [1, 128], BF16)
    ones32 = A("sb_ones32", [1, 128], F32)
    onescol = A("sb_onescol", [128, 1], BF16)
    gwT = A("sb_gwT", [128, KC * NE], F32)
    gb = A("sb_gb", [1, NE], F32)
    wgt = A("sb_wgt", [128, NT_OWN * NE], F32)
    lgall = A("sb_lgall", [128, NT_OWN * NE], F32)
    lgsc = A("sb_lgsc", [128, 4 * NT_OWN], F32)
    lg2 = A("sb_lg2", [128, NT_OWN * NE], F32)
    Gbf = A("sb_Gbf", [128, KC * C], BF16)
    mcol = A("sb_mcol", [128, KC], F32)
    mbf = A("sb_mbf", [128, KC], BF16)
    ccol = A("sb_ccol", [128, KC], F32)
    cbf = A("sb_cbf", [128, KC], BF16)
    rcol = A("sb_rcol", [128, KC], F32)
    rbf = A("sb_rbf", [128, KC], BF16)
    u1bf = A("sb_u1bf", [128, KC], BF16)
    Bbf = A("sb_Bbf", [128, C], BF16)
    Btbf = A("sb_Btbf", [128, C], BF16)
    bqcol = A("sb_bqcol", [128, KC], BF16)
    bpcol = A("sb_bpcol", [128, KC], F32)
    bkrow = A("sb_bkrow", [1, C], BF16)
    bvrow = A("sb_bvrow", [1, C], BF16)
    bvrow32 = A("sb_bvrow32", [1, C], F32)
    urow = A("sb_urow", [1, C], F32)
    urowbf = A("sb_urowbf", [1, C], BF16)
    rkrow = A("sb_rkrow", [1, C], BF16)
    lnb = A("sb_ln", [128, 16], F32)
    sh1bf = A("sb_sh1bf", [128, 512], BF16)
    b0col = A("sb_b0col", [128, 2 * HC], F32)
    b1col = A("sb_b1col", [128, 2 * HC], F32)
    b2rbf = A("sb_b2rbf", [1, 2 * C], BF16)
    t32a = A("sb_t32a", [128, 1024], F32)
    t32b = A("sb_t32b", [128, 1024], F32)
    t32c = A("sb_t32c", [128, 1024], F32)
    t32d = A("sb_t32d", [128, 1024], F32)

    ARENA = 144 * KB
    arena = A("arena", [128, ARENA], U8)
    sv = _carve(nc, arena, {
        "xnbf":  (0, [128, NT_BATCH * C], BF16),
        "E":     (0 * KB, [128, KC * C], F32),
        "Ebf":   (32 * KB, [128, KC * C], BF16),
        "wkTbf": (48 * KB, [128, KC * C], BF16),
        "wvTbf": (64 * KB, [128, KC * C], BF16),
        "wpTbf": (80 * KB, [128, KC * C], BF16),
        "wqbf":  (96 * KB, [128, KC * C], BF16),
        "bigA":  (112 * KB, [128, KC * C], BF16),
        "bigB":  (128 * KB, [128, KC * C], BF16),
        "xnTbf3": (48 * KB, [128, KC * TOK], BF16),
        "xnTbf": (0 * KB, [128, KC * TOK], BF16),
        "w0bf":  (32 * KB, [128, KC * HF], BF16),
        "w1bf":  (64 * KB, [128, KC * HF], BF16),
        "w2bf4": (96 * KB, [128, HC * C], BF16),
        "actbf": (128 * KB, [128, HC * 512], BF16),
    })
    xnbf = sv["xnbf"]
    E, Ebf = sv["E"], sv["Ebf"]
    wkTbf, wvTbf, wpTbf, wqbf = sv["wkTbf"], sv["wvTbf"], sv["wpTbf"], sv["wqbf"]
    bigA, bigB = sv["bigA"], sv["bigB"]
    xnTbf3, xnTbf = sv["xnTbf3"], sv["xnTbf"]
    w0bf, w1bf, w2bf4, actbf = sv["w0bf"], sv["w1bf"], sv["w2bf4"], sv["actbf"]

    psum = nc.alloc_psum_tensor("psum", [128, 4096], F32)

    sems = {s: nc.alloc_semaphore(name="sem_" + s) for s in SEM_NAMES}

    def emit(c: Ctx):
        mk = {}

        def mark(name):
            mk[name] = dict(c.n)

        def wmk(engname, name, sem):
            if c.cur == engname:
                c.eng.wait_ge(c.sems[sem], mk[name][sem])

        guard = {b: [] for b in range(8)}

        def bank(b, cols=512):
            return psum[:, b * 512:b * 512 + cols]

        def bank_bf(b):
            return psum[:, b * 512:b * 512 + 64].bitcast(BF16)

        def bg_wait(b):
            if c.cur == "tensor":
                for sem, cnt in guard[b]:
                    c.eng.wait_ge(c.sems[sem], cnt)
            guard[b] = []

        def bg_set(b, sem):
            guard[b].append((sem, c.n[sem]))

        c.dma("sync", eye32[:], eye32_d.ap(), sem="ld0")
        c.dma("sync", eyebf[:], eyebf_d.ap(), sem="ld0")
        c.dma("sync", gwT[:].rearrange("p (c e) -> p c e", e=NE),
              gwT_d.ap().rearrange("(c p) e -> p c e", p=128), sem="ld0")
        c.dma("sync", gb[:], gb_d.ap(), sem="ld0")
        p0_ld = c.n["ld0"]
        c.v("memset", onesbf[:], 1.0)
        c.v("memset", ones32[:], 1.0)
        c.v("memset", onescol[:], 1.0, sem="dv")
        mark("p0")

        xv = [t32a, t32b]
        acsq = {}
        ldi = {}
        for i in range(NT_BATCH + 1):
            if i < NT_BATCH:
                if i >= 2:
                    c.wait("sync", "dv", mk[f"xn_{i - 2}"]["dv"])
                ldp = "lda" if i % 2 == 0 else "ldb"
                c.dma("sync", xv[i % 2][:],
                      x_batch.ap()[i * 128:(i + 1) * 128, :], sem=ldp)
                ldi[i] = c.n[ldp]
            if i >= 1:
                j = i - 1
                ownj = j < NT_OWN
                b8 = (j % 2) * 8
                xj = xv[j % 2]
                c.wait("vector", "ac", acsq[j])
                c.v("tensor_scalar", lnb[:, b8 + 3:b8 + 4], lnb[:, b8 + 2:b8 + 3],
                    1.0 / C, LN_EPS, op0=ALU.mult, op1=ALU.add)
                c.v("tensor_tensor", lnb[:, b8 + 4:b8 + 5], lnb[:, b8 + 1:b8 + 2],
                    lnb[:, b8 + 1:b8 + 2], op=ALU.mult)
                c.v("tensor_sub", lnb[:, b8 + 5:b8 + 6], lnb[:, b8 + 3:b8 + 4],
                    lnb[:, b8 + 4:b8 + 5], sem="dv")
                c.wait("scalar", "dv", c.n["dv"])
                if j >= 2:
                    c.wait("scalar", "dv", mk[f"xn_{j - 2}"]["dv"])
                c.s("activation", lnb[:, b8 + 6:b8 + 7], lnb[:, b8 + 5:b8 + 6],
                    AF.Sqrt, sem="ac")
                c.wait("vector", "ac", c.n["ac"])
                c.v("reciprocal", lnb[:, b8 + 7:b8 + 8], lnb[:, b8 + 6:b8 + 7])
                if 1 <= j <= NT_OWN:
                    wmk("vector", f"tr_{j - 1}", "pe")
                if ownj:
                    c.v("tensor_scalar", t32c[:], xj[:], lnb[:, b8 + 1:b8 + 2],
                        lnb[:, b8 + 7:b8 + 8],
                        op0=ALU.subtract, op1=ALU.mult, sem="dv")
                    mark(f"xn_{j}")
                    c.wait("scalar", "dv", c.n["dv"])
                    c.s("copy", xnbf[:, j * C:(j + 1) * C], t32c[:], sem="ac")
                else:
                    c.v("tensor_scalar", xnbf[:, j * C:(j + 1) * C], xj[:],
                        lnb[:, b8 + 1:b8 + 2], lnb[:, b8 + 7:b8 + 8],
                        op0=ALU.subtract, op1=ALU.mult, sem="dv")
                    mark(f"xn_{j}")
                if j >= NT_OWN:
                    c.wait("tensor", "dv", mk[f"xn_{j}"]["dv"])
                    if j == NT_OWN:
                        for b in range(8):
                            bg_wait(b)
                    for ic in range(KC):
                        c.mm(bank(ic),
                             xnbf[:, j * C + ic * 128: j * C + (ic + 1) * 128],
                             xnbf[:, j * C: j * C + 512],
                             start=(j == NT_OWN), stop=False)
                if ownj:
                    c.wait("tensor", "ld0", p0_ld)
                    c.wait("tensor", "dv", mk[f"xn_{j}"]["dv"])
                    for g in range(2):
                        b = 4 + g
                        bg_wait(b)
                        for f4 in range(4):
                            f = g * 4 + f4
                            c.tr(psum[:, b * 512 + f4 * 128:b * 512 + f4 * 128 + 128],
                                 t32c[:, f * 128:(f + 1) * 128], eye32[:, :],
                                 sem="pe" if f4 == 3 else None)
                        c.wait("vector", "pe", c.n["pe"])
                        c.v("tensor_copy", t32d[:, g * 512:(g + 1) * 512],
                            bank(b), sem="dv")
                        bg_set(b, "dv")
                    mark(f"tr_{j}")
                    c.wait("tensor", "dv", c.n["dv"])
                    bg_wait(3)
                    for f in range(KC):
                        c.mm(psum[:, 3 * 512:3 * 512 + NE],
                             t32d[:, f * 128:(f + 1) * 128],
                             gwT[:, f * NE:(f + 1) * NE],
                             start=(f == 0), stop=False)
                    c.mm(psum[:, 3 * 512:3 * 512 + NE], ones32[:], gb[:],
                         start=False, stop=True, sem="pe")
                    mark(f"gate_{j}")
                    c.wait("scalar", "pe", c.n["pe"])
                    c.s("copy", lgall[:, j * NE:(j + 1) * NE],
                        psum[:, 3 * 512:3 * 512 + NE], sem="ac")
                    bg_set(3, "ac")
                    c.wait("sync", "dv", c.n["dv"])
                    for f in range(KC):
                        c.dma("sync", xnT_sp.ap()[f, :, j * 128:(j + 1) * 128],
                              t32d[:, f * 128:(f + 1) * 128], sem="st0")
                    mark(f"spill_{j}")
            if i < NT_BATCH:
                b8i = (i % 2) * 8
                ldp = "lda" if i % 2 == 0 else "ldb"
                c.wait("scalar", ldp, ldi[i])
                if 1 <= i <= NT_OWN:
                    wmk("scalar", f"gate_{i - 1}", "pe")
                    wmk("scalar", f"spill_{i - 1}", "st0")
                if i >= 2:
                    c.wait("scalar", "dv", mk[f"xn_{i - 2}"]["dv"])
                c.s("activation", t32d[:], xv[i % 2][:], AF.Square,
                    accum_out=lnb[:, b8i + 2:b8i + 3], sem="ac")
                acsq[i] = c.n["ac"]
                c.wait("vector", ldp, ldi[i])
                c.v("reduce_sum", lnb[:, b8i + 0:b8i + 1], xv[i % 2][:],
                    axis=AX.X)
                c.v("tensor_scalar_mul", lnb[:, b8i + 1:b8i + 2],
                    lnb[:, b8i + 0:b8i + 1], 1.0 / C)
        mark("p1_tiles")

        LG3 = lgall[:].rearrange("p (t e) -> p t e", e=NE)
        W3 = wgt[:].rearrange("p (t e) -> p t e", e=NE)
        L23 = lg2[:].rearrange("p (t e) -> p t e", e=NE)
        m1c = lgsc[:, 0 * NT_OWN:1 * NT_OWN]
        m2c = lgsc[:, 1 * NT_OWN:2 * NT_OWN]
        d21c = lgsc[:, 2 * NT_OWN:3 * NT_OWN]
        invc = lgsc[:, 3 * NT_OWN:4 * NT_OWN]
        c.wait("vector", "ac", c.n["ac"])
        c.v("tensor_reduce", m1c.unsqueeze(2), LG3, axis=AX.X, op=ALU.max)
        c.v("scalar_tensor_tensor", L23, m1c.unsqueeze(2).broadcast_to(
            (128, NT_OWN, NE)), -1.0, LG3, op0=ALU.mult, op1=ALU.add)
        c.v("tensor_scalar", L23, L23, 0.0, -1e30, op0=ALU.is_ge, op1=ALU.mult)
        c.v("tensor_add", L23, L23, LG3)
        c.v("tensor_reduce", m2c.unsqueeze(2), L23, axis=AX.X, op=ALU.max,
            sem="dv")
        c.wait("vector", "dv", c.n["dv"])
        c.v("tensor_sub", d21c, m2c, m1c, sem="dv")
        c.wait("scalar", "dv", c.n["dv"])
        c.s("activation", d21c, d21c, AF.Exp, sem="ac")
        c.wait("vector", "ac", c.n["ac"])
        c.v("tensor_scalar_add", invc, d21c, 1.0)
        c.v("reciprocal", invc, invc)
        c.v("tensor_tensor", L23, L23, m2c.unsqueeze(2).broadcast_to(
            (128, NT_OWN, NE)), op=ALU.is_ge)
        c.v("tensor_tensor", L23, L23, d21c.unsqueeze(2).broadcast_to(
            (128, NT_OWN, NE)), op=ALU.mult)
        c.v("tensor_tensor", W3, LG3, m1c.unsqueeze(2).broadcast_to(
            (128, NT_OWN, NE)), op=ALU.is_ge)
        c.v("tensor_add", W3, W3, L23)
        c.v("tensor_tensor", W3, W3, invc.unsqueeze(2).broadcast_to(
            (128, NT_OWN, NE)), op=ALU.mult, sem="dv")
        mark("wgt_done")

        c.wait("tensor", "ac", mk["p1_tiles"]["ac"])
        for t in range(NT_OWN):
            for ic in range(KC):
                c.mm(bank(ic),
                     xnbf[:, t * C + ic * 128: t * C + (ic + 1) * 128],
                     xnbf[:, t * C: t * C + 512],
                     start=False, stop=(t == NT_OWN - 1),
                     sem="pe" if t == NT_OWN - 1 else None)
        c.wait("scalar", "pe", c.n["pe"])
        for ic in range(KC):
            c.s("copy", Gbf[:, ic * C: ic * C + 512], bank(ic), sem="ac")
        for b in range(8):
            bg_set(b, "ac")
        for b in range(8):
            bg_wait(b)
        for t in range(NT_BATCH):
            for ic in range(KC):
                c.mm(bank(ic),
                     xnbf[:, t * C + ic * 128: t * C + (ic + 1) * 128],
                     xnbf[:, t * C + 512: t * C + 1024],
                     start=(t == 0), stop=(t == NT_BATCH - 1),
                     sem="pe" if t == NT_BATCH - 1 else None)
        c.wait("scalar", "pe", c.n["pe"])
        for ic in range(KC):
            c.s("copy", Gbf[:, ic * C + 512: ic * C + 1024], bank(ic), sem="ac")
        for b in range(8):
            bg_set(b, "ac")
        mark("G")
        c.wait("tensor", "dv", mk["p0"]["dv"])
        for ic in range(KC):
            bg_wait(0)
            for t in range(NT_BATCH):
                c.mm(psum[:, 0:1],
                     xnbf[:, t * C + ic * 128: t * C + (ic + 1) * 128],
                     onescol[:],
                     start=(t == 0), stop=(t == NT_BATCH - 1),
                     sem="pe" if t == NT_BATCH - 1 else None)
            c.wait("vector", "pe", c.n["pe"])
            c.v("tensor_copy", mcol[:, ic:ic + 1], psum[:, 0:1], sem="dv")
            bg_set(0, "dv")
        c.wait("scalar", "dv", c.n["dv"])
        c.s("copy", mbf[:], mcol[:], sem="ac")
        mark("p1_done")

        if "G" in dbg_d:
            c.wait("gpsimd", "ac", mk["G"]["ac"])
            c.dma("gpsimd", dbg_d["G"].ap().rearrange("(c p) n -> p c n", p=128),
                  Gbf[:].rearrange("p (c n) -> p c n", n=C), sem="dg")
        if "m" in dbg_d:
            c.wait("gpsimd", "dv", mk["p1_done"]["dv"])
            c.dma("gpsimd", dbg_d["m"].ap().rearrange("(c p) one -> p c one", p=128),
                  mcol[:].unsqueeze(2), sem="dg", allow_slow_non_contiguous=True)
        if "wgt" in dbg_d:
            c.wait("gpsimd", "dv", mk["p1_done"]["dv"])
            c.dma("gpsimd", dbg_d["wgt"].ap().rearrange("(t p) e -> p t e", p=128),
                  wgt[:].rearrange("p (t e) -> p t e", e=NE), sem="dg")
        if "xnT" in dbg_d:
            c.wait("gpsimd", "st0", mk["p1_done"]["st0"])
            c.dma("gpsimd", dbg_d["xnT"].ap().rearrange("(c p) t -> c p t", p=128),
                  xnT_sp.ap(), sem="dg")

        if phases < 2:
            c.wait("sync", "st0", c.n["st0"])
            if dbg_d:
                c.wait("sync", "dg", c.n["dg"])
            return
        c.wait("vector", "pe", mk["p1_done"]["pe"])
        c.v("memset", E[:], 0.0)
        c.v("memset", ccol[:], 0.0, sem="dv")
        mark("p2_init")

        wlc = {}
        ldcc = {}

        def load_weight(name, dst, srcd, li2):
            c.wait("gpsimd", "wl", c.n["wl"])
            c.dma("gpsimd", dst[:].rearrange("p (cc n) -> p cc n", n=C),
                  srcd.ap()[li2].rearrange("(cc p) n -> p cc n", p=128),
                  sem="wl")
            wlc[(name, li2)] = c.n["wl"]

        def load_bq(li2):
            c.wait("gpsimd", "wl", c.n["wl"])
            for f in range(KC):
                c.dma("gpsimd", bqcol[:, f:f + 1],
                      bqkv_d.ap()[li2:li2 + 1, f * 128:(f + 1) * 128]
                      .rearrange("one p -> p one"), sem="wl")
            wlc[("bq", li2)] = c.n["wl"]

        def load_bkv(li2):
            c.wait("gpsimd", "wl", c.n["wl"])
            c.dma("gpsimd", bkrow[0:1, 0:C],
                  bqkv_d.ap()[li2:li2 + 1, C:2 * C], sem="wl")
            c.dma("gpsimd", bvrow[0:1, 0:C],
                  bqkv_d.ap()[li2:li2 + 1, 2 * C:3 * C], sem="wl")
            wlc[("bkv", li2)] = c.n["wl"]

        def load_biases_g(li2):
            load_bq(li2)
            load_bkv(li2)

        def load_bv32(li2):
            c.wait("sync", "ldc", c.n["ldc"])
            c.dma("sync", bvrow32[0:1, 0:C],
                  bqkv_d.ap()[li2:li2 + 1, 2 * C:3 * C], sem="ldc")
            wlc[("bv32", li2)] = c.n["ldc"]

        def load_bp(li2):
            c.wait("sync", "ldc", c.n["ldc"])
            for f in range(KC):
                c.dma("sync", bpcol[:, f:f + 1],
                      bproj_d.ap()[li2:li2 + 1, f * 128:(f + 1) * 128]
                      .rearrange("one p -> p one"), sem="ldc")
            wlc[("bp", li2)] = c.n["ldc"]

        def load_biases_s(li2):
            load_bv32(li2)
            load_bp(li2)
            ldcc[li2] = c.n["ldc"]

        for li in range(nlayers):
            s_ = 0
            sr = 0
            if li == 0:
                c.wait("gpsimd", "pe", mk["p1_done"]["pe"])
                c.wait("gpsimd", "dv", mk["p2_init"]["dv"])
                for name, dst, srcd in (("wk", wkTbf, wkT_d), ("wv", wvTbf, wvT_d),
                                        ("wp", wpTbf, wpT_d), ("wq", wqbf, wq_d)):
                    load_weight(name, dst, srcd, 0)
                load_biases_g(0)
                load_biases_s(0)
            wl_l = None

            c.wait("scalar", "dv", c.n["dv"])
            c.s("copy", cbf[:], ccol[:], sem="ac")
            mark(f"L{li}_snap")
            if p2_until == "snap":
                break

            if li == 0:
                c.v("tensor_copy", rcol[:], mcol[:], sem="dv")
            else:
                c.wait("tensor", "ac", c.n["ac"])
                bg_wait(5)
                for jc in range(KC):
                    for ic in range(KC):
                        c.mm(psum[:, 5 * 512 + jc:5 * 512 + jc + 1],
                             Ebf[:, ic * C + jc * 128: ic * C + (jc + 1) * 128],
                             mbf[:, ic:ic + 1],
                             start=(ic == 0), stop=(ic == KC - 1),
                             sem="pe" if ic == KC - 1 else None)
                c.wait("vector", "pe", c.n["pe"])
                c.v("tensor_add", rcol[:], psum[:, 5 * 512:5 * 512 + KC], mcol[:])
                c.v("scalar_tensor_tensor", rcol[:], ccol[:], TB, rcol[:],
                    op0=ALU.mult, op1=ALU.add, sem="dv")
                bg_set(5, "dv")
            c.wait("scalar", "dv", c.n["dv"])
            c.s("copy", rbf[:], rcol[:], sem="ac")
            ac_rbf = c.n["ac"]

            c.wait("tensor", "wl", max(wlc[(nm, li)] for nm in ("wk", "wv", "wp", "wq", "bq", "bkv")))
            c.wait("tensor", "ac", ac_rbf)
            bg_wait(6)
            bg_wait(7)
            for nh_ in range(2):
                for ic in range(KC):
                    c.mm(psum[0:1, (6 + nh_) * 512:(6 + nh_) * 512 + 512],
                         rbf[:, ic:ic + 1],
                         wkTbf[:, ic * C + nh_ * 512: ic * C + nh_ * 512 + 512],
                         start=(ic == 0), stop=(ic == KC - 1),
                         sem="pe" if ic == KC - 1 else None)
            c.wait("scalar", "pe", c.n["pe"])
            c.s("copy", rkrow[:], psum[0:1, 6 * 512:8 * 512], sem="ac")
            bg_set(6, "ac")
            bg_set(7, "ac")
            bg_wait(6)
            bg_wait(7)
            for nh_ in range(2):
                for ic in range(KC):
                    c.mm(psum[0:1, (6 + nh_) * 512:(6 + nh_) * 512 + 512],
                         rbf[:, ic:ic + 1],
                         wvTbf[:, ic * C + nh_ * 512: ic * C + nh_ * 512 + 512],
                         start=(ic == 0), stop=(ic == KC - 1),
                         sem="pe" if ic == KC - 1 else None)
            c.wait("vector", "pe", c.n["pe"])
            c.wait("vector", "ldc", wlc[("bv32", li)])
            c.v("scalar_tensor_tensor", urow[:], bvrow32[0:1, sr:sr + C], TB,
                psum[0:1, 6 * 512:8 * 512], op0=ALU.mult, op1=ALU.add, sem="dv")
            bg_set(6, "dv")
            bg_set(7, "dv")
            c.wait("scalar", "dv", c.n["dv"])
            c.s("copy", urowbf[:], urow[:], sem="ac")
            mark(f"L{li}_rows")
            if p2_until == "rows":
                break

            def stream(lhsT_at, rhs_at, evac, tag):
                pp = [0]
                for oc in range(KC):
                    for jh in range(2):
                        b = pp[0]
                        bg_wait(b)
                        for kc in range(KC):
                            c.mm(bank(b), lhsT_at(kc, oc), rhs_at(kc, jh),
                                 start=(kc == 0), stop=(kc == KC - 1),
                                 sem="pe" if kc == KC - 1 else None)
                        evac(oc, jh, b)
                        pp[0] = (pp[0] + 1) % 4
                mark(tag)

            def ev_m1(oc, jh, b):
                c.wait("vector", "pe", c.n["pe"])
                c.v("tensor_tensor",
                    bigA[:, oc * C + jh * 512: oc * C + jh * 512 + 512],
                    bank(b), Gbf[:, oc * C + jh * 512: oc * C + jh * 512 + 512],
                    op=ALU.add, sem="dv")
                bg_set(b, "dv")

            if li == 0:
                c.wait("vector", "ac", mk["p1_done"]["ac"])
                c.v("tensor_copy", bigB[:], Gbf[:], sem="dv")
            else:
                stream(lambda kc, oc: Gbf[:, kc * C + oc * 128: kc * C + (oc + 1) * 128],
                       lambda kc, jh: Ebf[:, kc * C + jh * 512: kc * C + jh * 512 + 512],
                       ev_m1, f"L{li}_m1")
            if p2_until == "m1":
                break

            def ev_m2(oc, jh, b):
                c.wait("vector", "pe", c.n["pe"])
                c.v("tensor_tensor",
                    bigB[:, oc * C + jh * 512: oc * C + jh * 512 + 512],
                    bank(b), bigA[:, oc * C + jh * 512: oc * C + jh * 512 + 512],
                    op=ALU.add, sem="dv")
                bg_set(b, "dv")

            if li > 0:
                c.wait("tensor", "dv", c.n["dv"])
                stream(lambda kc, oc: Ebf[:, kc * C + oc * 128: kc * C + (oc + 1) * 128],
                       lambda kc, jh: bigA[:, kc * C + jh * 512: kc * C + jh * 512 + 512],
                       ev_m2, f"L{li}_m2")
            if p2_until == "m2":
                break

            def ev_m3(oc, jh, b):
                c.wait("scalar", "pe", c.n["pe"])
                c.s("copy", bigA[:, oc * C + jh * 512: oc * C + jh * 512 + 512],
                    bank(b), sem="ac")
                bg_set(b, "ac")

            c.wait("tensor", "dv", c.n["dv"])
            stream(lambda kc, jc: bigB[:, kc * C + jc * 128: kc * C + (jc + 1) * 128],
                   lambda kc, oh: wkTbf[:, kc * C + oh * 512: kc * C + oh * 512 + 512],
                   ev_m3, f"L{li}_m3")
            if li + 1 < nlayers:
                wmk("gpsimd", f"L{li}_m3", "pe")
                load_weight("wk", wkTbf, wkT_d, li + 1)
            if p2_until == "m3":
                break

            c.wait("tensor", "ac", c.n["ac"])
            bg_wait(2)
            bg_wait(3)
            for h in range(NH):
                pair, r0 = h // 2, (h % 2) * 64
                o0 = pair * 128 + r0
                oap = psum[r0:r0 + 64, 1024 + o0:1024 + o0 + 64]
                for jc in range(KC):
                    c.mm(oap, bigA[:, jc * C + h * 64: jc * C + h * 64 + 64],
                         wvTbf[:, jc * C + h * 64: jc * C + h * 64 + 64],
                         start=(jc == 0), stop=False, tp=(0, r0))
                c.mm(oap, rkrow[0:1, h * 64:h * 64 + 64],
                     bvrow[0:1, sr + h * 64:sr + h * 64 + 64],
                     start=False, stop=False, tp=(0, r0))
                c.mm(oap, bkrow[0:1, sr + h * 64:sr + h * 64 + 64],
                     urowbf[0:1, h * 64:h * 64 + 64], start=False, stop=True,
                     tp=(0, r0), sem="pe")
            c.wait("scalar", "pe", c.n["pe"])
            c.s("activation", Bbf[:], psum[:, 1024:2048], AF.Identity,
                scale=SCALE, sem="ac")
            bg_set(2, "ac")
            bg_set(3, "ac")
            mark(f"L{li}_B")
            if li + 1 < nlayers:
                wmk("gpsimd", f"L{li}_B", "pe")
                load_weight("wv", wvTbf, wvT_d, li + 1)
                load_bkv(li + 1)
            elif p2_until is None and phases >= 3:
                wmk("gpsimd", f"L{li}_B", "pe")
                c.wait("gpsimd", "st0", mk["p1_done"]["st0"])
                c.wait("gpsimd", "wl", c.n["wl"])
                c.dma("gpsimd",
                      xnTbf3[:].rearrange("p (cc t) -> p cc t", t=TOK),
                      xnT_sp.ap().rearrange("cc p t -> p cc t"), sem="wl")
                wlc[("xnT3", 0)] = c.n["wl"]
            if p2_until == "B":
                break

            c.wait("tensor", "ac", c.n["ac"])
            bg_wait(4)
            for f in range(KC):
                c.tr(psum[:, 4 * 512 + f * 64:4 * 512 + f * 64 + 64].bitcast(BF16),
                     Bbf[:, f * 128:(f + 1) * 128], eyebf[:, :],
                     sem="pe" if f == KC - 1 else None)
            c.wait("vector", "pe", c.n["pe"])
            c.v("tensor_copy", Btbf[:], bank(4).bitcast(BF16), sem="dv")
            bg_set(4, "dv")
            mark(f"L{li}_Bt")
            if p2_until == "Bt":
                break

            bg_wait(2)
            for h in range(NH):
                pair, r0 = h // 2, (h % 2) * 64
                c.mm(psum[r0:r0 + 64, 2 * 512 + pair:2 * 512 + pair + 1],
                     Bbf[r0:r0 + 64, pair * 128 + r0:pair * 128 + r0 + 64],
                     bqcol[r0:r0 + 64, pair:pair + 1],
                     start=True, stop=True, tp=(r0, r0),
                     sem="pe" if h == NH - 1 else None)
            c.wait("scalar", "pe", c.n["pe"])
            c.s("copy", u1bf[:], psum[:, 2 * 512:2 * 512 + KC], sem="ac")
            bg_set(2, "ac")
            mark(f"L{li}_u1")
            if li + 1 < nlayers:
                wmk("gpsimd", f"L{li}_u1", "pe")
                load_bq(li + 1)
                c.wait("sync", "dv", mk[f"L{li}_rows"]["dv"])
                load_bv32(li + 1)
            if p2_until == "u1":
                break

            c.wait("tensor", "dv", c.n["dv"])
            for pair in range(KC):
                b0_, b1_ = 0, 1
                bg_wait(b0_)
                bg_wait(b1_)
                for half in range(2):
                    bb = b0_ if half == 0 else b1_
                    for sub in range(2):
                        r0 = sub * 64
                        c.mm(psum[r0:r0 + 64, bb * 512:bb * 512 + 512],
                             Btbf[r0:r0 + 64,
                                  pair * 128 + r0:pair * 128 + r0 + 64],
                             wpTbf[r0:r0 + 64,
                                   pair * C + half * 512:pair * C + half * 512 + 512],
                             start=True, stop=True, tp=(r0, r0),
                             sem="pe" if sub == 1 else None)
                c.wait("scalar", "pe", c.n["pe"])
                c.s("copy", bigB[:, pair * C:pair * C + 512], bank(b0_), sem="ac")
                c.s("copy", bigB[:, pair * C + 512:(pair + 1) * C], bank(b1_),
                    sem="ac")
                bg_set(b0_, "ac")
                bg_set(b1_, "ac")
            mark(f"L{li}_m5")
            if p2_until == "m5":
                break

            def ev_m6(ic, jh, b):
                c.wait("scalar", "pe", c.n["pe"])
                c.s("copy", bigA[:, ic * C + jh * 512: ic * C + jh * 512 + 512],
                    bank(b), sem="ac")
                c.wait("vector", "ac", c.n["ac"])
                c.v("tensor_tensor",
                    E[:, ic * C + jh * 512: ic * C + jh * 512 + 512],
                    E[:, ic * C + jh * 512: ic * C + jh * 512 + 512],
                    bank(b), op=ALU.add, sem="dv")
                bg_set(b, "dv")

            c.wait("tensor", "ac", c.n["ac"])
            stream(lambda oc, ic: wqbf[:, oc * C + ic * 128: oc * C + (ic + 1) * 128],
                   lambda oc, jh: bigB[:, oc * C + jh * 512: oc * C + jh * 512 + 512],
                   ev_m6, f"L{li}_m6")
            if li + 1 < nlayers:
                wmk("gpsimd", f"L{li}_m6", "pe")
                load_weight("wq", wqbf, wq_d, li + 1)
            if p2_until == "m6":
                break

            c.wait("tensor", "ac", c.n["ac"])
            bg_wait(2)
            if li > 0:
                for jc in range(KC):
                    for kc in range(KC):
                        c.mm(psum[:, 2 * 512 + 8 + jc:2 * 512 + 8 + jc + 1],
                             bigA[:, kc * C + jc * 128: kc * C + (jc + 1) * 128],
                             cbf[:, kc:kc + 1],
                             start=(kc == 0), stop=(kc == KC - 1),
                             sem="pe" if kc == KC - 1 else None)
            for jc in range(KC):
                for yc in range(KC):
                    c.mm(psum[:, 2 * 512 + 16 + jc:2 * 512 + 16 + jc + 1],
                         wpTbf[:, yc * C + jc * 128: yc * C + (jc + 1) * 128],
                         u1bf[:, yc:yc + 1],
                         start=(yc == 0), stop=(yc == KC - 1),
                         sem="pe" if yc == KC - 1 else None)
            c.wait("vector", "pe", c.n["pe"])
            c.wait("vector", "ldc", wlc[("bp", li)])
            if li > 0:
                c.v("tensor_add", ccol[:], ccol[:], psum[:, 2 * 512 + 8:2 * 512 + 16])
            c.v("tensor_add", ccol[:], ccol[:], psum[:, 2 * 512 + 16:2 * 512 + 24])
            c.v("tensor_add", ccol[:], ccol[:], bpcol[:, s_:s_ + KC], sem="dv")
            bg_set(2, "dv")
            mark(f"L{li}_c")
            if li + 1 < nlayers:
                wmk("gpsimd", f"L{li}_c", "pe")
                load_weight("wp", wpTbf, wpT_d, li + 1)
                c.wait("sync", "dv", mk[f"L{li}_c"]["dv"])
                load_bp(li + 1)
            if p2_until == "c":
                break

            if li == 0:
                c.wait("scalar", "dv", c.n["dv"])
                for ic in range(KC):
                    c.s("copy", Ebf[:, ic * C:(ic + 1) * C],
                        E[:, ic * C:(ic + 1) * C], sem="ac")
                mark(f"L{li}_done")
                continue
            for kc in range(KC):
                b = 4 + (kc % 2)
                bg_wait(b)
                for ic in range(KC):
                    c.tr(psum[:, b * 512 + ic * 64:b * 512 + ic * 64 + 64]
                         .bitcast(BF16),
                         Ebf[:, ic * C + kc * 128: ic * C + (kc + 1) * 128],
                         eyebf[:, :], sem="pe" if ic == KC - 1 else None)
                c.wait("vector", "pe", c.n["pe"])
                c.v("tensor_copy", bigB[:, kc * C:(kc + 1) * C],
                    bank(b).bitcast(BF16), sem="dv")
                bg_set(b, "dv")
            mark(f"L{li}_Et")
            if p2_until == "Et":
                break

            def ev_m7(ic, jh, b):
                c.wait("vector", "pe", c.n["pe"])
                c.v("tensor_tensor",
                    E[:, ic * C + jh * 512: ic * C + jh * 512 + 512],
                    E[:, ic * C + jh * 512: ic * C + jh * 512 + 512],
                    bank(b), op=ALU.add, sem="dv")
                bg_set(b, "dv")
                if jh == 1:
                    c.wait("scalar", "dv", c.n["dv"])
                    c.s("copy", Ebf[:, ic * C:(ic + 1) * C],
                        E[:, ic * C:(ic + 1) * C], sem="ac")

            c.wait("tensor", "dv", c.n["dv"])
            c.wait("tensor", "ac", c.n["ac"])
            stream(lambda kc, ic: bigB[:, kc * C + ic * 128: kc * C + (ic + 1) * 128],
                   lambda kc, jh: bigA[:, kc * C + jh * 512: kc * C + jh * 512 + 512],
                   ev_m7, f"L{li}_m7")
            mark(f"L{li}_done")

        if "E" in dbg_d:
            c.wait("gpsimd", "dv", c.n["dv"])
            c.wait("gpsimd", "wl", c.n["wl"])
            c.dma("gpsimd", dbg_d["E"].ap().rearrange("(cc p) n -> p cc n", p=128),
                  E[:].rearrange("p (cc n) -> p cc n", n=C), sem="dg")
        if "c" in dbg_d:
            c.wait("gpsimd", "dv", c.n["dv"])
            c.wait("gpsimd", "dg", c.n["dg"])
            c.dma("gpsimd", dbg_d["c"].ap().rearrange("(cc p) one -> p cc one", p=128),
                  ccol[:].unsqueeze(2), sem="dg", allow_slow_non_contiguous=True)

        if phases < 3:
            c.wait("sync", "st0", c.n["st0"])
            if dbg_d:
                c.wait("sync", "dg", c.n["dg"])
            return
        if ("xnT3", 0) not in wlc:
            c.wait("gpsimd", "pe", c.n["pe"])
            c.wait("gpsimd", "st0", mk["p1_done"]["st0"])
            c.wait("gpsimd", "wl", c.n["wl"])
            c.dma("gpsimd", xnTbf3[:].rearrange("p (cc t) -> p cc t", t=TOK),
                  xnT_sp.ap().rearrange("cc p t -> p cc t"), sem="wl")
            wlc[("xnT3", 0)] = c.n["wl"]
        wl_p3 = wlc[("xnT3", 0)]
        if nlayers == 0:
            c.wait("scalar", "dv", c.n["dv"])
            c.s("copy", Ebf[:], E[:], sem="ac")
        mark("p3_ebf")
        wlc4 = {}
        ldc4 = {}

        def load_w01(e2):
            c.wait("gpsimd", "wl", c.n["wl"])
            for dst, srcd in ((w0bf, w0T_d), (w1bf, w1T_d)):
                c.dma("gpsimd", dst[:].rearrange("p (cc n) -> p cc n", n=HF),
                      srcd.ap()[e2].rearrange("(cc p) n -> p cc n", p=128),
                      sem="wl")
            c.dma("gpsimd", b2rbf[0:1, (e2 % 2) * C:(e2 % 2) * C + C],
                  b2_d.ap()[e2:e2 + 1, :], sem="wl")
            wlc4[("w01", e2)] = c.n["wl"]

        def load_w2e(e2):
            c.wait("gpsimd", "wl", c.n["wl"])
            c.dma("gpsimd", w2bf4[:].rearrange("p (cc n) -> p cc n", n=C),
                  w2T_d.ap()[e2].rearrange("(cc p) n -> p cc n", p=128),
                  sem="wl")
            wlc4[("w2", e2)] = c.n["wl"]

        def load_b01(e2):
            s4 = (e2 % 2) * HC
            c.wait("sync", "ldc", c.n["ldc"])
            for hcc in range(HC):
                c.dma("sync", b0col[:, s4 + hcc:s4 + hcc + 1],
                      b0_d.ap()[e2:e2 + 1, hcc * 128:(hcc + 1) * 128]
                      .rearrange("one p -> p one"), sem="ldc")
                c.dma("sync", b1col[:, s4 + hcc:s4 + hcc + 1],
                      b1_d.ap()[e2:e2 + 1, hcc * 128:(hcc + 1) * 128]
                      .rearrange("one p -> p one"), sem="ldc")
            ldc4[e2] = c.n["ldc"]

        if phases >= 4:
            c.wait("gpsimd", "ac", mk["p3_ebf"]["ac"])
            if "E" in dbg_d or "c" in dbg_d:
                c.wait("gpsimd", "dg", c.n["dg"])
            c.wait("gpsimd", "wl", c.n["wl"])
            c.dma("gpsimd", xnTbf[:].rearrange("p (cc t) -> p cc t", t=TOK),
                  xnT_sp.ap().rearrange("cc p t -> p cc t"), sem="wl")
            wl_p4 = c.n["wl"]
            load_w2e(0)
            load_b01(0)
        for tc in range(TOK // 512):
            for oc in range(KC):
                it = tc * KC + oc
                s4_ = it % 4
                xt = [t32a, t32b][s4_ // 2]
                xh = (s4_ % 2) * 512
                ldp = ("lda", "ldb", "w2a", "w2b")[s4_]
                stp = "st0" if it % 2 == 0 else "st1"
                et = [t32c, t32d][s4_ // 2]
                if it >= 4:
                    c.wait("sync", "dv", mk[f"p3_ev_{it - 4}"]["dv"])
                else:
                    c.wait("sync", "dv", mk["p1_tiles"]["dv"])
                c.dma("sync", xt[:, xh:xh + 512],
                      xnT_sp.ap()[oc, :, tc * 512:(tc + 1) * 512], sem=ldp)
                ld_it = c.n[ldp]
                b = s4_
                bg_wait(b)
                c.wait("tensor", "wl", wl_p3)
                c.wait("tensor", "ac", mk["p3_ebf"]["ac"])
                for ic in range(KC):
                    c.mm(bank(b),
                         Ebf[:, ic * C + oc * 128: ic * C + (oc + 1) * 128],
                         xnTbf3[:, ic * TOK + tc * 512: ic * TOK + tc * 512 + 512],
                         start=(ic == 0), stop=(ic == KC - 1),
                         sem="pe" if ic == KC - 1 else None)
                c.wait("vector", "pe", c.n["pe"])
                c.wait("vector", ldp, ld_it)
                if it >= 2:
                    c.wait("vector", stp, mk[f"p3_st_{it - 2}"][stp])
                c.v("scalar_tensor_tensor", et[:, xh:xh + 512], bank(b),
                    ccol[:, oc:oc + 1], xt[:, xh:xh + 512],
                    op0=ALU.add, op1=ALU.add, sem="dv")
                bg_set(b, "dv")
                mark(f"p3_ev_{it}")
                c.wait("sync", "dv", c.n["dv"])
                c.dma("sync", sT_sp.ap()[oc, :, tc * 512:(tc + 1) * 512],
                      et[:, xh:xh + 512], sem=stp)
                mark(f"p3_st_{it}")
        mark("p3_done")
        if "sT" in dbg_d:
            c.wait("gpsimd", "st0", c.n["st0"])
            c.wait("gpsimd", "st1", c.n["st1"])
            c.wait("gpsimd", "wl", c.n["wl"])
            c.dma("gpsimd", dbg_d["sT"].ap().rearrange("(cc p) t -> cc p t", p=128),
                  sT_sp.ap(), sem="dg")

        if phases < 4:
            c.wait("sync", "st0", c.n["st0"])
            c.wait("sync", "st1", c.n["st1"])
            if dbg_d:
                c.wait("sync", "dg", c.n["dg"])
            return
        c.wait("gpsimd", "pe", mk["p3_done"]["pe"])

        for e in range(nexperts):
            s4 = (e % 2) * HC
            sb2 = (e % 2) * C
            if e == 0:
                load_w01(0)
            else:
                wmk("gpsimd", f"p4_mm_{e - 1}", "pe")
                load_w2e(e)
            wl_e = wlc4[("w01", e)]
            ld_e = ldc4[e]

            for tq in range(4):
                t0 = tq * 512
                c.wait("tensor", "wl", wlc4[("w01", e)])
                c.wait("tensor", "wl", wl_p4)
                for hcc in range(HC):
                    b0_ = hcc % 2
                    b1_ = 2 + hcc % 2
                    bg_wait(b0_)
                    for dc in range(KC):
                        c.mm(bank(b0_),
                             w0bf[:, dc * HF + hcc * 128: dc * HF + (hcc + 1) * 128],
                             xnTbf[:, dc * TOK + t0: dc * TOK + t0 + 512],
                             start=(dc == 0), stop=(dc == KC - 1),
                             sem="pe" if dc == KC - 1 else None)
                    pe_h0 = c.n["pe"]
                    bg_wait(b1_)
                    for dc in range(KC):
                        c.mm(bank(b1_),
                             w1bf[:, dc * HF + hcc * 128: dc * HF + (hcc + 1) * 128],
                             xnTbf[:, dc * TOK + t0: dc * TOK + t0 + 512],
                             start=(dc == 0), stop=(dc == KC - 1),
                             sem="pe" if dc == KC - 1 else None)
                    c.wait("scalar", "pe", pe_h0)
                    c.wait("scalar", "ldc", ld_e)
                    c.s("activation", actbf[:, hcc * 512:(hcc + 1) * 512],
                        bank(b0_), AF.Identity, bias=b0col[:, s4 + hcc:s4 + hcc + 1],
                        sem="ac")
                    bg_set(b0_, "ac")
                    c.wait("scalar", "pe", c.n["pe"])
                    c.wait("scalar", "dv", c.n["dv"])
                    c.s("activation", sh1bf[:], bank(b1_), AF.Silu,
                        bias=b1col[:, s4 + hcc:s4 + hcc + 1], sem="ac")
                    bg_set(b1_, "ac")
                    c.wait("vector", "ac", c.n["ac"])
                    c.v("tensor_mul", actbf[:, hcc * 512:(hcc + 1) * 512],
                        actbf[:, hcc * 512:(hcc + 1) * 512], sh1bf[:], sem="dv")
                mark(f"p4_h_{e}_{tq}")
                if tq == 3 and e + 1 < nexperts:
                    wmk("gpsimd", f"p4_h_{e}_3", "pe")
                    load_w01(e + 1)
                    wmk("sync", f"p4_h_{e}_3", "ac")
                    load_b01(e + 1)

                for b in range(8):
                    bg_wait(b)
                c.wait("tensor", "dv", c.n["dv"])
                c.wait("tensor", "wl", wlc4[("w2", e)])
                for hcc in range(HC):
                    for mt in range(4):
                        for dch in range(2):
                            c.mm(bank(mt * 2 + dch),
                                 actbf[:, hcc * 512 + mt * 128:
                                       hcc * 512 + (mt + 1) * 128],
                                 w2bf4[:, hcc * C + dch * 512:
                                       hcc * C + dch * 512 + 512],
                                 start=(hcc == 0), stop=False,
                                 sem="pe" if (mt == 3 and dch == 1) else None)
                for mt in range(4):
                    for dch in range(2):
                        c.mm(bank(mt * 2 + dch), onesbf[:],
                             b2rbf[0:1, sb2 + dch * 512:sb2 + dch * 512 + 512],
                             start=False, stop=True, sem="pe")
                for mt in range(4):
                    tglob = tq * 4 + mt
                    dst_t = t32a if mt % 2 == 0 else t32b
                    c.wait("vector", "pe", c.n["pe"])
                    wcol = wgt[:, tglob * NE + e: tglob * NE + e + 1]
                    wap = "wa0" if mt % 2 == 0 else "wa1"
                    if e == 0 and tq == 0 and mt < 2:
                        for ss in ("lda", "ldb", "w2a", "w2b"):
                            c.wait("vector", ss, mk["p3_done"][ss])
                    if mt >= 2 or e > 0 or tq > 0:
                        c.wait("vector", wap, c.n[wap])
                    for dch in range(2):
                        c.v("tensor_scalar_mul",
                            dst_t[:, dch * 512:dch * 512 + 512],
                            bank(mt * 2 + dch), wcol, sem="dv")
                        bg_set(mt * 2 + dch, "dv")
                    c.wait("gpsimd", "dv", c.n["dv"])
                    c.wait("gpsimd", wap, c.n[wap])
                    if e > 0:
                        c.dma("gpsimd", acc_d.ap()[tglob], dst_t[:],
                              sem=wap, accum_op=ALU.add)
                    else:
                        c.dma("gpsimd", acc_d.ap()[tglob], dst_t[:], sem=wap)
                    mark(f"p4_acc_{e}_{tq}_{mt}")
            mark(f"p4_mm_{e}")
            mark(f"p4_act_{e}")

        c.wait("sync", "st0", c.n["st0"])
        c.wait("sync", "st1", c.n["st1"])
        for tglob in range(NT_OWN):
            mtp = "wa0" if (tglob % 4) % 2 == 0 else "wa1"
            accmark = f"p4_acc_{nexperts - 1}_{tglob // 4}_{tglob % 4}"
            wmk("sync", accmark, mtp)
            if tglob >= 1:
                wmk("sync", f"p4_fin_{tglob - 1}", "dv")
            c.dma("sync", t32c[:], acc_d.ap()[tglob], sem="ldb")
            ldacc = c.n["ldb"]
            c.wait("scalar", "ldb", ldacc)
            if tglob == 0:
                c.wait("scalar", "st0", mk["p3_done"]["st0"])
                c.wait("scalar", "st1", mk["p3_done"]["st1"])
            c.s("activation", t32c[:], t32c[:], AF.Silu, sem="ac")
            mark(f"p4_g_{tglob}")
            if tglob >= 1:
                wmk("sync", f"p4_str_{tglob - 1}", "pe")
                wmk("sync", f"p4_out_{tglob - 1}", "st0")
            for oc in range(KC):
                c.dma("sync", t32d[:, oc * 128:(oc + 1) * 128],
                      sT_sp.ap()[oc, :, tglob * 128:(tglob + 1) * 128],
                      sem="lda")
            ld_s = c.n["lda"]
            c.wait("tensor", "lda", ld_s)
            for g in range(2):
                b = 4 + g
                bg_wait(b)
                for f4 in range(4):
                    oc = g * 4 + f4
                    c.tr(psum[:, b * 512 + f4 * 128:b * 512 + f4 * 128 + 128],
                         t32d[:, oc * 128:(oc + 1) * 128], eye32[:, :],
                         sem="pe" if f4 == 3 else None)
                c.wait("vector", "pe", c.n["pe"])
                if g == 0 and tglob == 0:
                    c.wait("vector", "st0", mk["p3_done"]["st0"])
                    c.wait("vector", "st1", mk["p3_done"]["st1"])
                if g == 0 and tglob >= 1:
                    wmk("vector", f"p4_out_{tglob - 1}", "st0")
                c.v("tensor_copy", t32d[:, g * 512:(g + 1) * 512],
                    bank(b), sem="dv")
                bg_set(b, "dv")
            mark(f"p4_str_{tglob}")
            c.wait("vector", "ac", mk[f"p4_g_{tglob}"]["ac"])
            c.v("tensor_mul", t32d[:], t32d[:], t32c[:], sem="dv")
            mark(f"p4_mul_{tglob}")
            c.wait("sync", "dv", mk[f"p4_mul_{tglob}"]["dv"])
            c.dma("sync", t32c[:],
                  x_batch.ap()[tglob * 128:(tglob + 1) * 128, :], sem="ldb")
            ld_x = c.n["ldb"]
            c.wait("vector", "ldb", ld_x)
            c.v("tensor_add", t32d[:], t32d[:], t32c[:], sem="dv")
            mark(f"p4_fin_{tglob}")
            c.wait("sync", "dv", c.n["dv"])
            c.dma("sync", out_d.ap()[tglob * 128:(tglob + 1) * 128, :],
                  t32d[:], sem="st0")
            mark(f"p4_out_{tglob}")
        c.wait("sync", "st0", c.n["st0"])
        c.wait("gpsimd", "st0", c.n["st0"])
        if dbg_d:
            c.wait("sync", "dg", c.n["dg"])

    with nc.Block() as block:
        @block.sync
        def _(eng):
            emit(Ctx(nc, "sync", eng, sems))

        @block.gpsimd
        def _(eng):
            emit(Ctx(nc, "gpsimd", eng, sems))

        @block.tensor
        def _(eng):
            emit(Ctx(nc, "tensor", eng, sems))

        @block.vector
        def _(eng):
            emit(Ctx(nc, "vector", eng, sems))

        @block.scalar
        def _(eng):
            emit(Ctx(nc, "scalar", eng, sems))

    return nc


def make_inputs(inputs):
    import ml_dtypes
    x = np.ascontiguousarray(np.asarray(inputs["L0_attn_inputs_embeds"],
                                        np.float32))
    Wqkv = np.asarray(inputs["L0_attn_Wqkv"], np.float32)
    com = {
        "eye32": np.eye(128, dtype=np.float32),
        "eyebf": np.eye(128, dtype=np.float32).astype(ml_dtypes.bfloat16),
        "wq": np.ascontiguousarray(Wqkv[:, 0:C, :]),
        "wkT": np.ascontiguousarray(Wqkv[:, C:2 * C, :].transpose(0, 2, 1)),
        "wvT": np.ascontiguousarray(Wqkv[:, 2 * C:3 * C, :].transpose(0, 2, 1)),
        "wpT": np.ascontiguousarray(
            np.asarray(inputs["L0_attn_Wproj"], np.float32).transpose(0, 2, 1)),
        "bqkv": np.ascontiguousarray(np.asarray(inputs["L0_attn_bqkv"],
                                                np.float32)),
        "bproj": np.ascontiguousarray(np.asarray(inputs["L0_attn_bproj"],
                                                 np.float32)),
        "gwT": np.ascontiguousarray(np.asarray(inputs["L0_ffn_gate_w"],
                                               np.float32).T),
        "gb": np.asarray(inputs["L0_ffn_gate_b"], np.float32).reshape(1, NE),
        "w0T": np.ascontiguousarray(np.asarray(inputs["L0_ffn_w0"],
                                               np.float32).transpose(0, 2, 1)),
        "w1T": np.ascontiguousarray(np.asarray(inputs["L0_ffn_w1"],
                                               np.float32).transpose(0, 2, 1)),
        "w2T": np.ascontiguousarray(np.asarray(inputs["L0_ffn_w2"],
                                               np.float32).transpose(0, 2, 1)),
        "b0": np.ascontiguousarray(np.asarray(inputs["L0_ffn_b0"], np.float32)),
        "b1": np.ascontiguousarray(np.asarray(inputs["L0_ffn_b1"], np.float32)),
        "b2": np.ascontiguousarray(np.asarray(inputs["L0_ffn_b2"], np.float32)),
    }
    in_maps = []
    for core in range(NCORE):
        bi, si = core // 4, core % 4
        xb = x[bi]
        own = xb[si * TOK:(si + 1) * TOK]
        rest = np.concatenate([xb[:si * TOK], xb[(si + 1) * TOK:]], axis=0)
        m = dict(com)
        m["x_batch"] = np.ascontiguousarray(np.concatenate([own, rest], axis=0))
        in_maps.append(m)
    return in_maps


def kernel(**inputs):
    in_maps = make_inputs(inputs)
    nc = build_nc()
    res = run_bass_kernel_spmd(nc, in_maps, core_ids=list(range(NCORE)))
    out = np.empty((B, T, C), np.float32)
    for core in range(NCORE):
        bi, si = core // 4, core % 4
        out[bi, si * TOK:(si + 1) * TOK, :] = res.results[core]["out"]
    return out


# version 11 (final)
# speedup vs baseline: 1.0513x; 1.0230x over previous
import numpy as np

import concourse.bass as bass
import concourse.mybir as mybir
from concourse.bass_utils import run_bass_kernel_spmd

F32 = mybir.dt.float32
BF16 = mybir.dt.bfloat16
U8 = mybir.dt.uint8
AF = mybir.ActivationFunctionType
ALU = mybir.AluOpType
AX = mybir.AxisListType

B, T, C = 2, 8192, 1024
NCORE = 8
TOK = T * B // NCORE
NT_OWN = TOK // 128
NT_BATCH = T // 128
KC = C // 128
NL = 12
NH, HD = 16, 64
NE, HF = 8, 2048
HC = HF // 128
SCALE = 0.125
LN_EPS = 1e-5
TB = float(T)

SEM_NAMES = ["ld0", "lda", "ldb", "ldc", "wl", "w2a", "w2b", "wa0", "wa1",
             "st0", "st1", "dg", "pe", "dv", "ac"]


class Ctx:

    def __init__(self, nc, cur, eng, sems):
        self.nc = nc
        self.cur = cur
        self.eng = eng
        self.sems = sems
        self.n = dict.fromkeys(SEM_NAMES, 0)

    def inc(self, instr, sem, amount):
        self.n[sem] += amount
        if instr is not None:
            instr.then_inc(self.sems[sem], amount)
        return instr

    def wait(self, engname, sem, value=None):
        if self.cur == engname:
            self.eng.wait_ge(self.sems[sem], self.n[sem] if value is None else value)

    def mm(self, out, lhsT, rhs, start=True, stop=True, tp=None, sem=None):
        ins = None
        if self.cur == "tensor":
            ins = self.nc.tensor.matmul(out, lhsT, rhs, start=start, stop=stop,
                                        tile_position=tp)
        if sem is not None:
            self.inc(ins, sem, 1)
        return ins

    def tr(self, out, in_, ident, tp=None, sem=None):
        ins = None
        if self.cur == "tensor":
            ins = self.nc.tensor.transpose(out, in_, ident, tile_position=tp)
        if sem is not None:
            self.inc(ins, sem, 1)
        return ins

    def dma(self, q, out, in_, sem=None, **kw):
        ins = None
        if self.cur == q:
            ins = getattr(self.nc, q).dma_start(out=out, in_=in_, **kw)
        if sem is not None:
            self.inc(ins, sem, 16)
        return ins

    def v(self, fn, *a, sem=None, **kw):
        ins = None
        if self.cur == "vector":
            self.eng.wait_ge(self.sems["dv"], self.n["dv"])
            ins = getattr(self.nc.vector, fn)(*a, **kw)
        self.inc(ins, "dv", 1)
        return ins

    def s(self, fn, *a, sem=None, **kw):
        ins = None
        if self.cur == "scalar":
            self.eng.wait_ge(self.sems["ac"], self.n["ac"])
            ins = getattr(self.nc.scalar, fn)(*a, **kw)
        self.inc(ins, "ac", 1)
        return ins


def _carve(nc, arena, plan):
    base = nc.lookup_mloc(arena).addr
    out = {}
    for name, (off, shape, dt) in plan.items():
        h = nc.virtual_sbuf_tensor(name, list(shape), dt)
        nc._tensor(h.name, list(h.shape), dt, type="SB")
        mloc = nc.lookup_mloc(h)
        mloc.allocated = True
        mloc.addr = base + off
        out[name] = h
    return out


def build_nc(dbg=(), nlayers=NL, nexperts=NE, phases=4, p2_until=None):
    nc = bass.Bass("TRN2", target_bir_lowering=False)
    KB = 1024

    x_batch = nc.dram_tensor("x_batch", [T, C], F32, kind="ExternalInput")
    eye32_d = nc.dram_tensor("eye32", [128, 128], F32, kind="ExternalInput")
    eyebf_d = nc.dram_tensor("eyebf", [128, 128], BF16, kind="ExternalInput")
    wq_d = nc.dram_tensor("wq", [NL, C, C], F32, kind="ExternalInput")
    wkT_d = nc.dram_tensor("wkT", [NL, C, C], F32, kind="ExternalInput")
    wvT_d = nc.dram_tensor("wvT", [NL, C, C], F32, kind="ExternalInput")
    wpT_d = nc.dram_tensor("wpT", [NL, C, C], F32, kind="ExternalInput")
    bqkv_d = nc.dram_tensor("bqkv", [NL, 3 * C], F32, kind="ExternalInput")
    bproj_d = nc.dram_tensor("bproj", [NL, C], F32, kind="ExternalInput")
    gwT_d = nc.dram_tensor("gwT", [C, NE], F32, kind="ExternalInput")
    gb_d = nc.dram_tensor("gb", [1, NE], F32, kind="ExternalInput")
    w0T_d = nc.dram_tensor("w0T", [NE, C, HF], F32, kind="ExternalInput")
    w1T_d = nc.dram_tensor("w1T", [NE, C, HF], F32, kind="ExternalInput")
    w2T_d = nc.dram_tensor("w2T", [NE, HF, C], F32, kind="ExternalInput")
    b0_d = nc.dram_tensor("b0", [NE, HF], F32, kind="ExternalInput")
    b1_d = nc.dram_tensor("b1", [NE, HF], F32, kind="ExternalInput")
    b2_d = nc.dram_tensor("b2", [NE, C], F32, kind="ExternalInput")
    out_d = nc.dram_tensor("out", [TOK, C], F32, kind="ExternalOutput")
    xnT_sp = nc.dram_tensor("xnT_sp", [KC, 128, TOK], F32)
    sT_sp = nc.dram_tensor("sT_sp", [KC, 128, TOK], F32)
    acc_d = nc.dram_tensor("acc_d", [NT_OWN, 128, C], F32)

    dbg_d = {}
    for name in dbg:
        shape = {"G": [C, C], "m": [C, 1], "E": [C, C], "c": [C, 1],
                 "wgt": [TOK, NE], "sT": [KC * 128, TOK],
                 "xnT": [KC * 128, TOK]}[name]
        dbg_d[name] = nc.dram_tensor("dbg_" + name, shape, F32,
                                     kind="ExternalOutput")

    A = nc.alloc_sbuf_tensor
    eye32 = A("sb_eye32", [128, 128], F32)
    eyebf = A("sb_eyebf", [128, 128], BF16)
    onesbf = A("sb_onesbf", [1, 128], BF16)
    ones32 = A("sb_ones32", [1, 128], F32)
    onescol = A("sb_onescol", [128, 1], BF16)
    gwT = A("sb_gwT", [128, KC * NE], F32)
    gb = A("sb_gb", [1, NE], F32)
    wgt = A("sb_wgt", [128, NT_OWN * NE], F32)
    lgall = A("sb_lgall", [128, NT_OWN * NE], F32)
    lgsc = A("sb_lgsc", [128, 4 * NT_OWN], F32)
    Gbf = A("sb_Gbf", [128, KC * C], BF16)
    mcol = A("sb_mcol", [128, KC], F32)
    mbf = A("sb_mbf", [128, KC], BF16)
    ccol = A("sb_ccol", [128, KC], F32)
    cbf = A("sb_cbf", [128, KC], BF16)
    rcol = A("sb_rcol", [128, KC], F32)
    rbf = A("sb_rbf", [128, KC], BF16)
    u1bf = A("sb_u1bf", [128, KC], BF16)
    Bbf = A("sb_Bbf", [128, C], BF16)
    Btbf = A("sb_Btbf", [128, C], BF16)
    bqcol = A("sb_bqcol", [128, KC], BF16)
    bpcol = A("sb_bpcol", [128, KC], F32)
    bkrow = A("sb_bkrow", [1, C], BF16)
    bvrow = A("sb_bvrow", [1, C], BF16)
    bvrow32 = A("sb_bvrow32", [1, C], F32)
    urow = A("sb_urow", [1, C], F32)
    urowbf = A("sb_urowbf", [1, C], BF16)
    rkrow = A("sb_rkrow", [1, C], BF16)
    lnb = A("sb_ln", [128, 16], F32)
    stats = A("sb_stats", [128, 48], F32)
    gbf = A("sb_gbf", [128, C], BF16)
    sh1bf = A("sb_sh1bf", [128, 2 * 512], BF16)
    b0col = A("sb_b0col", [128, 2 * HC], F32)
    b1col = A("sb_b1col", [128, 2 * HC], F32)
    b2rbf = A("sb_b2rbf", [1, 2 * C], BF16)
    t32a = A("sb_t32a", [128, 1024], F32)
    t32b = A("sb_t32b", [128, 1024], F32)
    t32c = A("sb_t32c", [128, 1024], F32)
    t32d = A("sb_t32d", [128, 1024], F32)

    ARENA = 144 * KB
    arena = A("arena", [128, ARENA], U8)
    sv = _carve(nc, arena, {
        "xnbf":  (0, [128, NT_BATCH * C], BF16),
        "E":     (0 * KB, [128, KC * C], F32),
        "Ebf":   (32 * KB, [128, KC * C], BF16),
        "wkTbf": (48 * KB, [128, KC * C], BF16),
        "wvTbf": (64 * KB, [128, KC * C], BF16),
        "wpTbf": (80 * KB, [128, KC * C], BF16),
        "wqbf":  (96 * KB, [128, KC * C], BF16),
        "bigA":  (112 * KB, [128, KC * C], BF16),
        "bigB":  (128 * KB, [128, KC * C], BF16),
        "xnTbf3": (48 * KB, [128, KC * TOK], BF16),
        "xnTbf": (0 * KB, [128, KC * TOK], BF16),
        "w0bf":  (32 * KB, [128, KC * HF], BF16),
        "w1bf":  (64 * KB, [128, KC * HF], BF16),
        "w2bf4": (96 * KB, [128, HC * C], BF16),
        "actbf": (128 * KB, [128, HC * 512], BF16),
    })
    xnbf = sv["xnbf"]
    E, Ebf = sv["E"], sv["Ebf"]
    wkTbf, wvTbf, wpTbf, wqbf = sv["wkTbf"], sv["wvTbf"], sv["wpTbf"], sv["wqbf"]
    bigA, bigB = sv["bigA"], sv["bigB"]
    xnTbf3, xnTbf = sv["xnTbf3"], sv["xnTbf"]
    w0bf, w1bf, w2bf4, actbf = sv["w0bf"], sv["w1bf"], sv["w2bf4"], sv["actbf"]

    psum = nc.alloc_psum_tensor("psum", [128, 4096], F32)

    sems = {s: nc.alloc_semaphore(name="sem_" + s) for s in SEM_NAMES}

    def emit(c: Ctx):
        mk = {}

        def mark(name):
            mk[name] = dict(c.n)

        def wmk(engname, name, sem):
            if c.cur == engname:
                c.eng.wait_ge(c.sems[sem], mk[name][sem])

        guard = {b: [] for b in range(8)}

        def bank(b, cols=512):
            return psum[:, b * 512:b * 512 + cols]

        def bank_bf(b):
            return psum[:, b * 512:b * 512 + 64].bitcast(BF16)

        def bg_wait(b):
            if c.cur == "tensor":
                for sem, cnt in guard[b]:
                    c.eng.wait_ge(c.sems[sem], cnt)
            guard[b] = []

        def bg_set(b, sem):
            guard[b].append((sem, c.n[sem]))

        c.dma("sync", eye32[:], eye32_d.ap(), sem="ld0")
        c.dma("sync", eyebf[:], eyebf_d.ap(), sem="ld0")
        c.dma("sync", gwT[:].rearrange("p (c e) -> p c e", e=NE),
              gwT_d.ap().rearrange("(c p) e -> p c e", p=128), sem="ld0")
        c.dma("sync", gb[:], gb_d.ap(), sem="ld0")
        p0_ld = c.n["ld0"]
        c.v("memset", onesbf[:], 1.0)
        c.v("memset", ones32[:], 1.0)
        c.v("memset", onescol[:], 1.0, sem="dv")
        mark("p0")

        xv = [t32a, t32b]
        acsq = {}
        ldi = {}
        for i in range(NT_BATCH + 1):
            if i < NT_BATCH:
                if i >= 2:
                    c.wait("sync", "dv", mk[f"xn_{i - 2}"]["dv"])
                ldp = "lda" if i % 2 == 0 else "ldb"
                c.dma("sync", xv[i % 2][:],
                      x_batch.ap()[i * 128:(i + 1) * 128, :], sem=ldp)
                ldi[i] = c.n[ldp]
            if i >= 1:
                j = i - 1
                ownj = j < NT_OWN
                b8 = (j % 2) * 8
                xj = xv[j % 2]
                c.wait("vector", "ac", acsq[j])
                c.v("tensor_scalar", lnb[:, b8 + 3:b8 + 4], lnb[:, b8 + 2:b8 + 3],
                    1.0 / C, LN_EPS, op0=ALU.mult, op1=ALU.add)
                c.v("tensor_tensor", lnb[:, b8 + 4:b8 + 5], lnb[:, b8 + 1:b8 + 2],
                    lnb[:, b8 + 1:b8 + 2], op=ALU.mult)
                c.v("tensor_sub", lnb[:, b8 + 5:b8 + 6], lnb[:, b8 + 3:b8 + 4],
                    lnb[:, b8 + 4:b8 + 5], sem="dv")
                c.wait("scalar", "dv", c.n["dv"])
                if j >= 2:
                    c.wait("scalar", "dv", mk[f"xn_{j - 2}"]["dv"])
                c.s("activation", lnb[:, b8 + 6:b8 + 7], lnb[:, b8 + 5:b8 + 6],
                    AF.Sqrt, sem="ac")
                c.wait("vector", "ac", c.n["ac"])
                c.v("reciprocal", lnb[:, b8 + 7:b8 + 8], lnb[:, b8 + 6:b8 + 7])
                if 1 <= j <= NT_OWN:
                    wmk("vector", f"tr_{j - 1}", "pe")
                if ownj:
                    c.v("tensor_copy", stats[:, j:j + 1], lnb[:, b8 + 1:b8 + 2])
                    c.v("tensor_copy", stats[:, 16 + j:17 + j],
                        lnb[:, b8 + 7:b8 + 8])
                    c.v("tensor_scalar", t32c[:], xj[:], lnb[:, b8 + 1:b8 + 2],
                        lnb[:, b8 + 7:b8 + 8],
                        op0=ALU.subtract, op1=ALU.mult, sem="dv")
                    mark(f"xn_{j}")
                    c.wait("scalar", "dv", c.n["dv"])
                    c.s("copy", xnbf[:, j * C:(j + 1) * C], t32c[:], sem="ac")
                else:
                    c.v("tensor_scalar", xnbf[:, j * C:(j + 1) * C], xj[:],
                        lnb[:, b8 + 1:b8 + 2], lnb[:, b8 + 7:b8 + 8],
                        op0=ALU.subtract, op1=ALU.mult, sem="dv")
                    mark(f"xn_{j}")
                if j >= NT_OWN:
                    c.wait("tensor", "dv", mk[f"xn_{j}"]["dv"])
                    if j == NT_OWN:
                        for b in range(8):
                            bg_wait(b)
                    for ic in range(KC):
                        c.mm(bank(ic),
                             xnbf[:, j * C + ic * 128: j * C + (ic + 1) * 128],
                             xnbf[:, j * C: j * C + 512],
                             start=(j == NT_OWN), stop=False)
                if ownj:
                    c.wait("tensor", "ld0", p0_ld)
                    c.wait("tensor", "dv", mk[f"xn_{j}"]["dv"])
                    for g in range(2):
                        b = 4 + g
                        bg_wait(b)
                        for f4 in range(4):
                            f = g * 4 + f4
                            c.tr(psum[:, b * 512 + f4 * 128:b * 512 + f4 * 128 + 128],
                                 t32c[:, f * 128:(f + 1) * 128], eye32[:, :],
                                 sem="pe" if f4 == 3 else None)
                        c.wait("vector", "pe", c.n["pe"])
                        c.v("tensor_copy", t32d[:, g * 512:(g + 1) * 512],
                            bank(b), sem="dv")
                        bg_set(b, "dv")
                    mark(f"tr_{j}")
                    c.wait("tensor", "dv", c.n["dv"])
                    bg_wait(3)
                    for f in range(KC):
                        c.mm(psum[:, 3 * 512:3 * 512 + NE],
                             t32d[:, f * 128:(f + 1) * 128],
                             gwT[:, f * NE:(f + 1) * NE],
                             start=(f == 0), stop=False)
                    c.mm(psum[:, 3 * 512:3 * 512 + NE], ones32[:], gb[:],
                         start=False, stop=True, sem="pe")
                    mark(f"gate_{j}")
                    c.wait("scalar", "pe", c.n["pe"])
                    c.s("copy", lgall[:, j * NE:(j + 1) * NE],
                        psum[:, 3 * 512:3 * 512 + NE], sem="ac")
                    bg_set(3, "ac")
                    c.wait("sync", "dv", c.n["dv"])
                    for f in range(KC):
                        c.dma("sync", xnT_sp.ap()[f, :, j * 128:(j + 1) * 128],
                              t32d[:, f * 128:(f + 1) * 128], sem="st0")
                    mark(f"spill_{j}")
            if i < NT_BATCH:
                b8i = (i % 2) * 8
                ldp = "lda" if i % 2 == 0 else "ldb"
                c.wait("scalar", ldp, ldi[i])
                if 1 <= i <= NT_OWN:
                    wmk("scalar", f"gate_{i - 1}", "pe")
                    wmk("scalar", f"spill_{i - 1}", "st0")
                if i >= 2:
                    c.wait("scalar", "dv", mk[f"xn_{i - 2}"]["dv"])
                c.s("activation", t32d[:], xv[i % 2][:], AF.Square,
                    accum_out=lnb[:, b8i + 2:b8i + 3], sem="ac")
                acsq[i] = c.n["ac"]
                c.wait("vector", ldp, ldi[i])
                c.v("reduce_sum", lnb[:, b8i + 0:b8i + 1], xv[i % 2][:],
                    axis=AX.X)
                c.v("tensor_scalar_mul", lnb[:, b8i + 1:b8i + 2],
                    lnb[:, b8i + 0:b8i + 1], 1.0 / C)
        mark("p1_tiles")

        LG3 = lgall[:].rearrange("p (t e) -> p t e", e=NE)
        W3 = wgt[:].rearrange("p (t e) -> p t e", e=NE)
        L23 = t32c[:, 0:NT_OWN * NE].rearrange("p (t e) -> p t e", e=NE)
        m1c = lgsc[:, 0 * NT_OWN:1 * NT_OWN]
        m2c = lgsc[:, 1 * NT_OWN:2 * NT_OWN]
        d21c = lgsc[:, 2 * NT_OWN:3 * NT_OWN]
        invc = lgsc[:, 3 * NT_OWN:4 * NT_OWN]
        c.wait("vector", "ac", c.n["ac"])
        c.v("tensor_reduce", m1c.unsqueeze(2), LG3, axis=AX.X, op=ALU.max)
        c.v("scalar_tensor_tensor", L23, m1c.unsqueeze(2).broadcast_to(
            (128, NT_OWN, NE)), -1.0, LG3, op0=ALU.mult, op1=ALU.add)
        c.v("tensor_scalar", L23, L23, 0.0, -1e30, op0=ALU.is_ge, op1=ALU.mult)
        c.v("tensor_add", L23, L23, LG3)
        c.v("tensor_reduce", m2c.unsqueeze(2), L23, axis=AX.X, op=ALU.max,
            sem="dv")
        c.wait("vector", "dv", c.n["dv"])
        c.v("tensor_sub", d21c, m2c, m1c, sem="dv")
        c.wait("scalar", "dv", c.n["dv"])
        c.s("activation", d21c, d21c, AF.Exp, sem="ac")
        c.wait("vector", "ac", c.n["ac"])
        c.v("tensor_scalar_add", invc, d21c, 1.0)
        c.v("reciprocal", invc, invc)
        c.v("tensor_tensor", L23, L23, m2c.unsqueeze(2).broadcast_to(
            (128, NT_OWN, NE)), op=ALU.is_ge)
        c.v("tensor_tensor", L23, L23, d21c.unsqueeze(2).broadcast_to(
            (128, NT_OWN, NE)), op=ALU.mult)
        c.v("tensor_tensor", W3, LG3, m1c.unsqueeze(2).broadcast_to(
            (128, NT_OWN, NE)), op=ALU.is_ge)
        c.v("tensor_add", W3, W3, L23)
        c.v("tensor_tensor", W3, W3, invc.unsqueeze(2).broadcast_to(
            (128, NT_OWN, NE)), op=ALU.mult, sem="dv")
        c.v("tensor_tensor", stats[:, 32:48], stats[:, 0:16], stats[:, 16:32],
            op=ALU.mult)
        c.v("tensor_scalar_mul", stats[:, 32:48], stats[:, 32:48], -1.0,
            sem="dv")
        mark("wgt_done")

        c.wait("tensor", "ac", mk["p1_tiles"]["ac"])
        for t in range(NT_OWN):
            for ic in range(KC):
                c.mm(bank(ic),
                     xnbf[:, t * C + ic * 128: t * C + (ic + 1) * 128],
                     xnbf[:, t * C: t * C + 512],
                     start=False, stop=(t == NT_OWN - 1),
                     sem="pe" if t == NT_OWN - 1 else None)
        c.wait("scalar", "pe", c.n["pe"])
        for ic in range(KC):
            c.s("copy", Gbf[:, ic * C: ic * C + 512], bank(ic), sem="ac")
        for b in range(8):
            bg_set(b, "ac")
        for b in range(8):
            bg_wait(b)
        for t in range(NT_BATCH):
            for ic in range(KC):
                c.mm(bank(ic),
                     xnbf[:, t * C + ic * 128: t * C + (ic + 1) * 128],
                     xnbf[:, t * C + 512: t * C + 1024],
                     start=(t == 0), stop=(t == NT_BATCH - 1),
                     sem="pe" if t == NT_BATCH - 1 else None)
        c.wait("scalar", "pe", c.n["pe"])
        for ic in range(KC):
            c.s("copy", Gbf[:, ic * C + 512: ic * C + 1024], bank(ic), sem="ac")
        for b in range(8):
            bg_set(b, "ac")
        mark("G")
        c.wait("tensor", "dv", mk["p0"]["dv"])
        for ic in range(KC):
            bg_wait(0)
            for t in range(NT_BATCH):
                c.mm(psum[:, 0:1],
                     xnbf[:, t * C + ic * 128: t * C + (ic + 1) * 128],
                     onescol[:],
                     start=(t == 0), stop=(t == NT_BATCH - 1),
                     sem="pe" if t == NT_BATCH - 1 else None)
            c.wait("vector", "pe", c.n["pe"])
            c.v("tensor_copy", mcol[:, ic:ic + 1], psum[:, 0:1], sem="dv")
            bg_set(0, "dv")
        c.wait("scalar", "dv", c.n["dv"])
        c.s("copy", mbf[:], mcol[:], sem="ac")
        mark("p1_done")

        if "G" in dbg_d:
            c.wait("gpsimd", "ac", mk["G"]["ac"])
            c.dma("gpsimd", dbg_d["G"].ap().rearrange("(c p) n -> p c n", p=128),
                  Gbf[:].rearrange("p (c n) -> p c n", n=C), sem="dg")
        if "m" in dbg_d:
            c.wait("gpsimd", "dv", mk["p1_done"]["dv"])
            c.dma("gpsimd", dbg_d["m"].ap().rearrange("(c p) one -> p c one", p=128),
                  mcol[:].unsqueeze(2), sem="dg", allow_slow_non_contiguous=True)
        if "wgt" in dbg_d:
            c.wait("gpsimd", "dv", mk["p1_done"]["dv"])
            c.dma("gpsimd", dbg_d["wgt"].ap().rearrange("(t p) e -> p t e", p=128),
                  wgt[:].rearrange("p (t e) -> p t e", e=NE), sem="dg")
        if "xnT" in dbg_d:
            c.wait("gpsimd", "st0", mk["p1_done"]["st0"])
            c.dma("gpsimd", dbg_d["xnT"].ap().rearrange("(c p) t -> c p t", p=128),
                  xnT_sp.ap(), sem="dg")

        if phases < 2:
            c.wait("sync", "st0", c.n["st0"])
            if dbg_d:
                c.wait("sync", "dg", c.n["dg"])
            return
        c.wait("vector", "pe", mk["p1_done"]["pe"])
        c.v("memset", E[:], 0.0)
        c.v("memset", ccol[:], 0.0, sem="dv")
        mark("p2_init")

        wlc = {}
        ldcc = {}

        def load_weight(name, dst, srcd, li2):
            c.wait("gpsimd", "wl", c.n["wl"])
            c.dma("gpsimd", dst[:].rearrange("p (cc n) -> p cc n", n=C),
                  srcd.ap()[li2].rearrange("(cc p) n -> p cc n", p=128),
                  sem="wl")
            wlc[(name, li2)] = c.n["wl"]

        def load_bq(li2):
            c.wait("gpsimd", "wl", c.n["wl"])
            for f in range(KC):
                c.dma("gpsimd", bqcol[:, f:f + 1],
                      bqkv_d.ap()[li2:li2 + 1, f * 128:(f + 1) * 128]
                      .rearrange("one p -> p one"), sem="wl")
            wlc[("bq", li2)] = c.n["wl"]

        def load_bkv(li2):
            c.wait("gpsimd", "wl", c.n["wl"])
            c.dma("gpsimd", bkrow[0:1, 0:C],
                  bqkv_d.ap()[li2:li2 + 1, C:2 * C], sem="wl")
            c.dma("gpsimd", bvrow[0:1, 0:C],
                  bqkv_d.ap()[li2:li2 + 1, 2 * C:3 * C], sem="wl")
            wlc[("bkv", li2)] = c.n["wl"]

        def load_biases_g(li2):
            load_bq(li2)
            load_bkv(li2)

        def load_bv32(li2):
            c.wait("sync", "ldc", c.n["ldc"])
            c.dma("sync", bvrow32[0:1, 0:C],
                  bqkv_d.ap()[li2:li2 + 1, 2 * C:3 * C], sem="ldc")
            wlc[("bv32", li2)] = c.n["ldc"]

        def load_bp(li2):
            c.wait("sync", "ldc", c.n["ldc"])
            for f in range(KC):
                c.dma("sync", bpcol[:, f:f + 1],
                      bproj_d.ap()[li2:li2 + 1, f * 128:(f + 1) * 128]
                      .rearrange("one p -> p one"), sem="ldc")
            wlc[("bp", li2)] = c.n["ldc"]

        def load_biases_s(li2):
            load_bv32(li2)
            load_bp(li2)
            ldcc[li2] = c.n["ldc"]

        for li in range(nlayers):
            s_ = 0
            sr = 0
            if li == 0:
                c.wait("gpsimd", "pe", mk["p1_done"]["pe"])
                c.wait("gpsimd", "dv", mk["p2_init"]["dv"])
                for name, dst, srcd in (("wk", wkTbf, wkT_d), ("wv", wvTbf, wvT_d),
                                        ("wp", wpTbf, wpT_d), ("wq", wqbf, wq_d)):
                    load_weight(name, dst, srcd, 0)
                load_biases_g(0)
                load_biases_s(0)
            wl_l = None

            c.wait("scalar", "dv", c.n["dv"])
            c.s("copy", cbf[:], ccol[:], sem="ac")
            mark(f"L{li}_snap")
            if p2_until == "snap":
                break

            if li == 0:
                c.v("tensor_copy", rcol[:], mcol[:], sem="dv")
            else:
                c.wait("tensor", "ac", c.n["ac"])
                bg_wait(5)
                for jc in range(KC):
                    for ic in range(KC):
                        c.mm(psum[:, 5 * 512 + jc:5 * 512 + jc + 1],
                             Ebf[:, ic * C + jc * 128: ic * C + (jc + 1) * 128],
                             mbf[:, ic:ic + 1],
                             start=(ic == 0), stop=(ic == KC - 1),
                             sem="pe" if ic == KC - 1 else None)
                c.wait("vector", "pe", c.n["pe"])
                c.v("tensor_add", rcol[:], psum[:, 5 * 512:5 * 512 + KC], mcol[:])
                c.v("scalar_tensor_tensor", rcol[:], ccol[:], TB, rcol[:],
                    op0=ALU.mult, op1=ALU.add, sem="dv")
                bg_set(5, "dv")
            c.wait("scalar", "dv", c.n["dv"])
            c.s("copy", rbf[:], rcol[:], sem="ac")
            ac_rbf = c.n["ac"]

            c.wait("tensor", "wl", max(wlc[(nm, li)] for nm in ("wk", "wv", "wp", "wq", "bq", "bkv")))
            c.wait("tensor", "ac", ac_rbf)
            bg_wait(6)
            bg_wait(7)
            for nh_ in range(2):
                for ic in range(KC):
                    c.mm(psum[0:1, (6 + nh_) * 512:(6 + nh_) * 512 + 512],
                         rbf[:, ic:ic + 1],
                         wkTbf[:, ic * C + nh_ * 512: ic * C + nh_ * 512 + 512],
                         start=(ic == 0), stop=(ic == KC - 1),
                         sem="pe" if ic == KC - 1 else None)
            c.wait("scalar", "pe", c.n["pe"])
            c.s("copy", rkrow[:], psum[0:1, 6 * 512:8 * 512], sem="ac")
            bg_set(6, "ac")
            bg_set(7, "ac")
            bg_wait(6)
            bg_wait(7)
            for nh_ in range(2):
                for ic in range(KC):
                    c.mm(psum[0:1, (6 + nh_) * 512:(6 + nh_) * 512 + 512],
                         rbf[:, ic:ic + 1],
                         wvTbf[:, ic * C + nh_ * 512: ic * C + nh_ * 512 + 512],
                         start=(ic == 0), stop=(ic == KC - 1),
                         sem="pe" if ic == KC - 1 else None)
            c.wait("vector", "pe", c.n["pe"])
            c.wait("vector", "ldc", wlc[("bv32", li)])
            c.v("scalar_tensor_tensor", urow[:], bvrow32[0:1, sr:sr + C], TB,
                psum[0:1, 6 * 512:8 * 512], op0=ALU.mult, op1=ALU.add, sem="dv")
            bg_set(6, "dv")
            bg_set(7, "dv")
            c.wait("scalar", "dv", c.n["dv"])
            c.s("copy", urowbf[:], urow[:], sem="ac")
            mark(f"L{li}_rows")
            if p2_until == "rows":
                break

            def stream(lhsT_at, rhs_at, evac, tag):
                pp = [0]
                for oc in range(KC):
                    for jh in range(2):
                        b = pp[0]
                        bg_wait(b)
                        for kc in range(KC):
                            c.mm(bank(b), lhsT_at(kc, oc), rhs_at(kc, jh),
                                 start=(kc == 0), stop=(kc == KC - 1),
                                 sem="pe" if kc == KC - 1 else None)
                        evac(oc, jh, b)
                        pp[0] = (pp[0] + 1) % 4
                mark(tag)

            def ev_m1(oc, jh, b):
                c.wait("vector", "pe", c.n["pe"])
                c.v("tensor_tensor",
                    bigA[:, oc * C + jh * 512: oc * C + jh * 512 + 512],
                    bank(b), Gbf[:, oc * C + jh * 512: oc * C + jh * 512 + 512],
                    op=ALU.add, sem="dv")
                bg_set(b, "dv")

            if li == 0:
                c.wait("vector", "ac", mk["p1_done"]["ac"])
                c.v("tensor_copy", bigB[:], Gbf[:], sem="dv")
            else:
                stream(lambda kc, oc: Gbf[:, kc * C + oc * 128: kc * C + (oc + 1) * 128],
                       lambda kc, jh: Ebf[:, kc * C + jh * 512: kc * C + jh * 512 + 512],
                       ev_m1, f"L{li}_m1")
            if p2_until == "m1":
                break

            def ev_m2(oc, jh, b):
                c.wait("vector", "pe", c.n["pe"])
                c.v("tensor_tensor",
                    bigB[:, oc * C + jh * 512: oc * C + jh * 512 + 512],
                    bank(b), bigA[:, oc * C + jh * 512: oc * C + jh * 512 + 512],
                    op=ALU.add, sem="dv")
                bg_set(b, "dv")

            if li > 0:
                c.wait("tensor", "dv", c.n["dv"])
                stream(lambda kc, oc: Ebf[:, kc * C + oc * 128: kc * C + (oc + 1) * 128],
                       lambda kc, jh: bigA[:, kc * C + jh * 512: kc * C + jh * 512 + 512],
                       ev_m2, f"L{li}_m2")
            if p2_until == "m2":
                break

            def ev_m3(oc, jh, b):
                c.wait("scalar", "pe", c.n["pe"])
                c.s("copy", bigA[:, oc * C + jh * 512: oc * C + jh * 512 + 512],
                    bank(b), sem="ac")
                bg_set(b, "ac")

            c.wait("tensor", "dv", c.n["dv"])
            stream(lambda kc, jc: bigB[:, kc * C + jc * 128: kc * C + (jc + 1) * 128],
                   lambda kc, oh: wkTbf[:, kc * C + oh * 512: kc * C + oh * 512 + 512],
                   ev_m3, f"L{li}_m3")
            if li + 1 < nlayers:
                wmk("gpsimd", f"L{li}_m3", "pe")
                load_weight("wk", wkTbf, wkT_d, li + 1)
            if p2_until == "m3":
                break

            c.wait("tensor", "ac", c.n["ac"])
            bg_wait(2)
            bg_wait(3)
            for h in range(NH):
                pair, r0 = h // 2, (h % 2) * 64
                o0 = pair * 128 + r0
                oap = psum[r0:r0 + 64, 1024 + o0:1024 + o0 + 64]
                for jc in range(KC):
                    c.mm(oap, bigA[:, jc * C + h * 64: jc * C + h * 64 + 64],
                         wvTbf[:, jc * C + h * 64: jc * C + h * 64 + 64],
                         start=(jc == 0), stop=False, tp=(0, r0))
                c.mm(oap, rkrow[0:1, h * 64:h * 64 + 64],
                     bvrow[0:1, sr + h * 64:sr + h * 64 + 64],
                     start=False, stop=False, tp=(0, r0))
                c.mm(oap, bkrow[0:1, sr + h * 64:sr + h * 64 + 64],
                     urowbf[0:1, h * 64:h * 64 + 64], start=False, stop=True,
                     tp=(0, r0), sem="pe")
            c.wait("scalar", "pe", c.n["pe"])
            c.s("activation", Bbf[:], psum[:, 1024:2048], AF.Identity,
                scale=SCALE, sem="ac")
            bg_set(2, "ac")
            bg_set(3, "ac")
            mark(f"L{li}_B")
            if li + 1 < nlayers:
                wmk("gpsimd", f"L{li}_B", "pe")
                load_weight("wv", wvTbf, wvT_d, li + 1)
                load_bkv(li + 1)
            elif p2_until is None and phases >= 3:
                wmk("gpsimd", f"L{li}_B", "pe")
                c.wait("gpsimd", "st0", mk["p1_done"]["st0"])
                c.wait("gpsimd", "wl", c.n["wl"])
                c.dma("gpsimd",
                      xnTbf3[:].rearrange("p (cc t) -> p cc t", t=TOK),
                      xnT_sp.ap().rearrange("cc p t -> p cc t"), sem="wl")
                wlc[("xnT3", 0)] = c.n["wl"]
            if p2_until == "B":
                break

            c.wait("tensor", "ac", c.n["ac"])
            bg_wait(4)
            for f in range(KC):
                c.tr(psum[:, 4 * 512 + f * 64:4 * 512 + f * 64 + 64].bitcast(BF16),
                     Bbf[:, f * 128:(f + 1) * 128], eyebf[:, :],
                     sem="pe" if f == KC - 1 else None)
            c.wait("vector", "pe", c.n["pe"])
            c.v("tensor_copy", Btbf[:], bank(4).bitcast(BF16), sem="dv")
            bg_set(4, "dv")
            mark(f"L{li}_Bt")
            if p2_until == "Bt":
                break

            bg_wait(2)
            for h in range(NH):
                pair, r0 = h // 2, (h % 2) * 64
                c.mm(psum[r0:r0 + 64, 2 * 512 + pair:2 * 512 + pair + 1],
                     Bbf[r0:r0 + 64, pair * 128 + r0:pair * 128 + r0 + 64],
                     bqcol[r0:r0 + 64, pair:pair + 1],
                     start=True, stop=True, tp=(r0, r0),
                     sem="pe" if h == NH - 1 else None)
            c.wait("scalar", "pe", c.n["pe"])
            c.s("copy", u1bf[:], psum[:, 2 * 512:2 * 512 + KC], sem="ac")
            bg_set(2, "ac")
            mark(f"L{li}_u1")
            if li + 1 < nlayers:
                wmk("gpsimd", f"L{li}_u1", "pe")
                load_bq(li + 1)
                c.wait("sync", "dv", mk[f"L{li}_rows"]["dv"])
                load_bv32(li + 1)
            if p2_until == "u1":
                break

            c.wait("tensor", "dv", c.n["dv"])
            for pair in range(KC):
                b0_, b1_ = 0, 1
                bg_wait(b0_)
                bg_wait(b1_)
                for half in range(2):
                    bb = b0_ if half == 0 else b1_
                    for sub in range(2):
                        r0 = sub * 64
                        c.mm(psum[r0:r0 + 64, bb * 512:bb * 512 + 512],
                             Btbf[r0:r0 + 64,
                                  pair * 128 + r0:pair * 128 + r0 + 64],
                             wpTbf[r0:r0 + 64,
                                   pair * C + half * 512:pair * C + half * 512 + 512],
                             start=True, stop=True, tp=(r0, r0),
                             sem="pe" if sub == 1 else None)
                c.wait("scalar", "pe", c.n["pe"])
                c.s("copy", bigB[:, pair * C:pair * C + 512], bank(b0_), sem="ac")
                c.s("copy", bigB[:, pair * C + 512:(pair + 1) * C], bank(b1_),
                    sem="ac")
                bg_set(b0_, "ac")
                bg_set(b1_, "ac")
            mark(f"L{li}_m5")
            if p2_until == "m5":
                break

            def ev_m6(ic, jh, b):
                c.wait("scalar", "pe", c.n["pe"])
                c.s("copy", bigA[:, ic * C + jh * 512: ic * C + jh * 512 + 512],
                    bank(b), sem="ac")
                c.wait("vector", "ac", c.n["ac"])
                c.v("tensor_tensor",
                    E[:, ic * C + jh * 512: ic * C + jh * 512 + 512],
                    E[:, ic * C + jh * 512: ic * C + jh * 512 + 512],
                    bank(b), op=ALU.add, sem="dv")
                bg_set(b, "dv")

            c.wait("tensor", "ac", c.n["ac"])
            stream(lambda oc, ic: wqbf[:, oc * C + ic * 128: oc * C + (ic + 1) * 128],
                   lambda oc, jh: bigB[:, oc * C + jh * 512: oc * C + jh * 512 + 512],
                   ev_m6, f"L{li}_m6")
            if li + 1 < nlayers:
                wmk("gpsimd", f"L{li}_m6", "pe")
                load_weight("wq", wqbf, wq_d, li + 1)
            if p2_until == "m6":
                break

            c.wait("tensor", "ac", c.n["ac"])
            bg_wait(2)
            if li > 0:
                for jc in range(KC):
                    for kc in range(KC):
                        c.mm(psum[:, 2 * 512 + 8 + jc:2 * 512 + 8 + jc + 1],
                             bigA[:, kc * C + jc * 128: kc * C + (jc + 1) * 128],
                             cbf[:, kc:kc + 1],
                             start=(kc == 0), stop=(kc == KC - 1),
                             sem="pe" if kc == KC - 1 else None)
            for jc in range(KC):
                for yc in range(KC):
                    c.mm(psum[:, 2 * 512 + 16 + jc:2 * 512 + 16 + jc + 1],
                         wpTbf[:, yc * C + jc * 128: yc * C + (jc + 1) * 128],
                         u1bf[:, yc:yc + 1],
                         start=(yc == 0), stop=(yc == KC - 1),
                         sem="pe" if yc == KC - 1 else None)
            c.wait("vector", "pe", c.n["pe"])
            c.wait("vector", "ldc", wlc[("bp", li)])
            if li > 0:
                c.v("tensor_add", ccol[:], ccol[:], psum[:, 2 * 512 + 8:2 * 512 + 16])
            c.v("tensor_add", ccol[:], ccol[:], psum[:, 2 * 512 + 16:2 * 512 + 24])
            c.v("tensor_add", ccol[:], ccol[:], bpcol[:, s_:s_ + KC], sem="dv")
            bg_set(2, "dv")
            mark(f"L{li}_c")
            if li + 1 < nlayers:
                wmk("gpsimd", f"L{li}_c", "pe")
                load_weight("wp", wpTbf, wpT_d, li + 1)
                c.wait("sync", "dv", mk[f"L{li}_c"]["dv"])
                load_bp(li + 1)
            if p2_until == "c":
                break

            if li == 0:
                c.wait("scalar", "dv", c.n["dv"])
                for ic in range(KC):
                    c.s("copy", Ebf[:, ic * C:(ic + 1) * C],
                        E[:, ic * C:(ic + 1) * C], sem="ac")
                mark(f"L{li}_done")
                continue
            for kc in range(KC):
                b = 4 + (kc % 2)
                bg_wait(b)
                for ic in range(KC):
                    c.tr(psum[:, b * 512 + ic * 64:b * 512 + ic * 64 + 64]
                         .bitcast(BF16),
                         Ebf[:, ic * C + kc * 128: ic * C + (kc + 1) * 128],
                         eyebf[:, :], sem="pe" if ic == KC - 1 else None)
                c.wait("vector", "pe", c.n["pe"])
                c.v("tensor_copy", bigB[:, kc * C:(kc + 1) * C],
                    bank(b).bitcast(BF16), sem="dv")
                bg_set(b, "dv")
            mark(f"L{li}_Et")
            if p2_until == "Et":
                break

            def ev_m7(ic, jh, b):
                c.wait("vector", "pe", c.n["pe"])
                c.v("tensor_tensor",
                    E[:, ic * C + jh * 512: ic * C + jh * 512 + 512],
                    E[:, ic * C + jh * 512: ic * C + jh * 512 + 512],
                    bank(b), op=ALU.add, sem="dv")
                bg_set(b, "dv")
                if jh == 1:
                    c.wait("scalar", "dv", c.n["dv"])
                    c.s("copy", Ebf[:, ic * C:(ic + 1) * C],
                        E[:, ic * C:(ic + 1) * C], sem="ac")

            c.wait("tensor", "dv", c.n["dv"])
            c.wait("tensor", "ac", c.n["ac"])
            stream(lambda kc, ic: bigB[:, kc * C + ic * 128: kc * C + (ic + 1) * 128],
                   lambda kc, jh: bigA[:, kc * C + jh * 512: kc * C + jh * 512 + 512],
                   ev_m7, f"L{li}_m7")
            mark(f"L{li}_done")

        if "E" in dbg_d:
            c.wait("gpsimd", "dv", c.n["dv"])
            c.wait("gpsimd", "wl", c.n["wl"])
            c.dma("gpsimd", dbg_d["E"].ap().rearrange("(cc p) n -> p cc n", p=128),
                  E[:].rearrange("p (cc n) -> p cc n", n=C), sem="dg")
        if "c" in dbg_d:
            c.wait("gpsimd", "dv", c.n["dv"])
            c.wait("gpsimd", "dg", c.n["dg"])
            c.dma("gpsimd", dbg_d["c"].ap().rearrange("(cc p) one -> p cc one", p=128),
                  ccol[:].unsqueeze(2), sem="dg", allow_slow_non_contiguous=True)

        if phases < 3:
            c.wait("sync", "st0", c.n["st0"])
            if dbg_d:
                c.wait("sync", "dg", c.n["dg"])
            return
        if ("xnT3", 0) not in wlc:
            c.wait("gpsimd", "pe", c.n["pe"])
            c.wait("gpsimd", "st0", mk["p1_done"]["st0"])
            c.wait("gpsimd", "wl", c.n["wl"])
            c.dma("gpsimd", xnTbf3[:].rearrange("p (cc t) -> p cc t", t=TOK),
                  xnT_sp.ap().rearrange("cc p t -> p cc t"), sem="wl")
            wlc[("xnT3", 0)] = c.n["wl"]
        wl_p3 = wlc[("xnT3", 0)]
        if nlayers == 0:
            c.wait("scalar", "dv", c.n["dv"])
            c.s("copy", Ebf[:], E[:], sem="ac")
        mark("p3_ebf")
        wlc4 = {}
        ldc4 = {}

        def load_w01(e2):
            c.wait("gpsimd", "wl", c.n["wl"])
            for dst, srcd in ((w0bf, w0T_d), (w1bf, w1T_d)):
                c.dma("gpsimd", dst[:].rearrange("p (cc n) -> p cc n", n=HF),
                      srcd.ap()[e2].rearrange("(cc p) n -> p cc n", p=128),
                      sem="wl")
            c.dma("gpsimd", b2rbf[0:1, (e2 % 2) * C:(e2 % 2) * C + C],
                  b2_d.ap()[e2:e2 + 1, :], sem="wl")
            wlc4[("w01", e2)] = c.n["wl"]

        def load_w2e(e2):
            c.wait("gpsimd", "wl", c.n["wl"])
            c.dma("gpsimd", w2bf4[:].rearrange("p (cc n) -> p cc n", n=C),
                  w2T_d.ap()[e2].rearrange("(cc p) n -> p cc n", p=128),
                  sem="wl")
            wlc4[("w2", e2)] = c.n["wl"]

        def load_b01(e2):
            s4 = (e2 % 2) * HC
            c.wait("sync", "ldc", c.n["ldc"])
            for hcc in range(HC):
                c.dma("sync", b0col[:, s4 + hcc:s4 + hcc + 1],
                      b0_d.ap()[e2:e2 + 1, hcc * 128:(hcc + 1) * 128]
                      .rearrange("one p -> p one"), sem="ldc")
                c.dma("sync", b1col[:, s4 + hcc:s4 + hcc + 1],
                      b1_d.ap()[e2:e2 + 1, hcc * 128:(hcc + 1) * 128]
                      .rearrange("one p -> p one"), sem="ldc")
            ldc4[e2] = c.n["ldc"]

        if phases >= 4:
            c.wait("gpsimd", "ac", mk["p3_ebf"]["ac"])
            if "E" in dbg_d or "c" in dbg_d:
                c.wait("gpsimd", "dg", c.n["dg"])
            c.wait("gpsimd", "wl", c.n["wl"])
            c.dma("gpsimd", xnTbf[:].rearrange("p (cc t) -> p cc t", t=TOK),
                  xnT_sp.ap().rearrange("cc p t -> p cc t"), sem="wl")
            wl_p4 = c.n["wl"]
            load_w2e(0)
            load_b01(0)
        for tc in range(TOK // 512):
            for oc in range(KC):
                it = tc * KC + oc
                s4_ = it % 4
                xh = (s4_ % 2) * 512
                stp = "st0" if it % 2 == 0 else "st1"
                et = [t32c, t32d][s4_ // 2]
                b = s4_
                bg_wait(b)
                c.wait("tensor", "wl", wl_p3)
                c.wait("tensor", "ac", mk["p3_ebf"]["ac"])
                for ic in range(KC):
                    c.mm(bank(b),
                         Ebf[:, ic * C + oc * 128: ic * C + (oc + 1) * 128],
                         xnTbf3[:, ic * TOK + tc * 512: ic * TOK + tc * 512 + 512],
                         start=(ic == 0), stop=(ic == KC - 1),
                         sem="pe" if ic == KC - 1 else None)
                c.wait("vector", "pe", c.n["pe"])
                if it < 4:
                    c.wait("vector", "st0", mk["p1_tiles"]["st0"])
                    c.wait("vector", "ac", mk["p3_ebf"]["ac"])
                if it >= 2:
                    c.wait("vector", stp, mk[f"p3_st_{it - 2}"][stp])
                c.v("tensor_single_scalar", et[:, xh:xh + 512], bank(b),
                    ccol[:, oc:oc + 1], op=ALU.add, sem="dv")
                bg_set(b, "dv")
                mark(f"p3_ev_{it}")
                c.wait("sync", "dv", c.n["dv"])
                c.dma("sync", sT_sp.ap()[oc, :, tc * 512:(tc + 1) * 512],
                      et[:, xh:xh + 512], sem=stp)
                mark(f"p3_st_{it}")
        mark("p3_done")
        if "sT" in dbg_d:
            c.wait("gpsimd", "st0", c.n["st0"])
            c.wait("gpsimd", "st1", c.n["st1"])
            c.wait("gpsimd", "wl", c.n["wl"])
            c.dma("gpsimd", dbg_d["sT"].ap().rearrange("(cc p) t -> cc p t", p=128),
                  sT_sp.ap(), sem="dg")

        if phases < 4:
            c.wait("sync", "st0", c.n["st0"])
            c.wait("sync", "st1", c.n["st1"])
            if dbg_d:
                c.wait("sync", "dg", c.n["dg"])
            return
        c.wait("gpsimd", "pe", mk["p3_done"]["pe"])

        sh1_hist = [None, None]
        for e in range(nexperts):
            s4 = (e % 2) * HC
            sb2 = (e % 2) * C
            if e == 0:
                load_w01(0)
            else:
                wmk("gpsimd", f"p4_mm_{e - 1}", "pe")
                load_w2e(e)
            wl_e = wlc4[("w01", e)]
            ld_e = ldc4[e]

            for tq in range(4):
                t0 = tq * 512
                c.wait("tensor", "wl", wlc4[("w01", e)])
                c.wait("tensor", "wl", wl_p4)
                for hcc in range(HC):
                    b0_ = hcc % 2
                    b1_ = 2 + hcc % 2
                    bg_wait(b0_)
                    for dc in range(KC):
                        c.mm(bank(b0_),
                             w0bf[:, dc * HF + hcc * 128: dc * HF + (hcc + 1) * 128],
                             xnTbf[:, dc * TOK + t0: dc * TOK + t0 + 512],
                             start=(dc == 0), stop=(dc == KC - 1),
                             sem="pe" if dc == KC - 1 else None)
                    pe_h0 = c.n["pe"]
                    bg_wait(b1_)
                    for dc in range(KC):
                        c.mm(bank(b1_),
                             w1bf[:, dc * HF + hcc * 128: dc * HF + (hcc + 1) * 128],
                             xnTbf[:, dc * TOK + t0: dc * TOK + t0 + 512],
                             start=(dc == 0), stop=(dc == KC - 1),
                             sem="pe" if dc == KC - 1 else None)
                    sh = sh1bf[:, (hcc % 2) * 512:(hcc % 2) * 512 + 512]
                    c.wait("scalar", "pe", pe_h0)
                    c.wait("scalar", "ldc", ld_e)
                    c.s("activation", actbf[:, hcc * 512:(hcc + 1) * 512],
                        bank(b0_), AF.Identity, bias=b0col[:, s4 + hcc:s4 + hcc + 1],
                        sem="ac")
                    bg_set(b0_, "ac")
                    c.wait("scalar", "pe", c.n["pe"])
                    if sh1_hist[hcc % 2] is not None:
                        c.wait("scalar", "dv", sh1_hist[hcc % 2])
                    c.s("activation", sh, bank(b1_), AF.Silu,
                        bias=b1col[:, s4 + hcc:s4 + hcc + 1], sem="ac")
                    bg_set(b1_, "ac")
                    c.wait("vector", "ac", c.n["ac"])
                    c.v("tensor_mul", actbf[:, hcc * 512:(hcc + 1) * 512],
                        actbf[:, hcc * 512:(hcc + 1) * 512], sh, sem="dv")
                    sh1_hist[hcc % 2] = c.n["dv"]
                mark(f"p4_h_{e}_{tq}")
                if tq == 3 and e + 1 < nexperts:
                    wmk("gpsimd", f"p4_h_{e}_3", "pe")
                    load_w01(e + 1)
                    wmk("sync", f"p4_h_{e}_3", "ac")
                    load_b01(e + 1)

                for b in range(8):
                    bg_wait(b)
                c.wait("tensor", "dv", c.n["dv"])
                c.wait("tensor", "wl", wlc4[("w2", e)])
                for hcc in range(HC):
                    for mt in range(4):
                        for dch in range(2):
                            c.mm(bank(mt * 2 + dch),
                                 actbf[:, hcc * 512 + mt * 128:
                                       hcc * 512 + (mt + 1) * 128],
                                 w2bf4[:, hcc * C + dch * 512:
                                       hcc * C + dch * 512 + 512],
                                 start=(hcc == 0), stop=False,
                                 sem="pe" if (mt == 3 and dch == 1) else None)
                for mt in range(4):
                    for dch in range(2):
                        c.mm(bank(mt * 2 + dch), onesbf[:],
                             b2rbf[0:1, sb2 + dch * 512:sb2 + dch * 512 + 512],
                             start=False, stop=True, sem="pe")
                for mt in range(4):
                    tglob = tq * 4 + mt
                    dst_t = t32a if mt % 2 == 0 else t32b
                    c.wait("vector", "pe", c.n["pe"])
                    wcol = wgt[:, tglob * NE + e: tglob * NE + e + 1]
                    wap = "wa0" if mt % 2 == 0 else "wa1"
                    if e == 0 and tq == 0 and mt < 2:
                        c.wait("vector", "lda", mk["p1_tiles"]["lda"])
                        c.wait("vector", "ldb", mk["p1_tiles"]["ldb"])
                        c.wait("vector", "ac", mk["p1_tiles"]["ac"])
                    if mt >= 2 or e > 0 or tq > 0:
                        c.wait("vector", wap, c.n[wap])
                    for dch in range(2):
                        c.v("tensor_scalar_mul",
                            dst_t[:, dch * 512:dch * 512 + 512],
                            bank(mt * 2 + dch), wcol, sem="dv")
                        bg_set(mt * 2 + dch, "dv")
                    c.wait("gpsimd", "dv", c.n["dv"])
                    c.wait("gpsimd", wap, c.n[wap])
                    if e > 0:
                        c.dma("gpsimd", acc_d.ap()[tglob], dst_t[:],
                              sem=wap, accum_op=ALU.add)
                    else:
                        c.dma("gpsimd", acc_d.ap()[tglob], dst_t[:], sem=wap)
                    mark(f"p4_acc_{e}_{tq}_{mt}")
            mark(f"p4_mm_{e}")
            mark(f"p4_act_{e}")

        c.wait("sync", "st0", c.n["st0"])
        c.wait("sync", "st1", c.n["st1"])
        for tglob in range(NT_OWN):
            mtp = "wa0" if (tglob % 4) % 2 == 0 else "wa1"
            accmark = f"p4_acc_{nexperts - 1}_{tglob // 4}_{tglob % 4}"
            wmk("sync", accmark, mtp)
            if tglob >= 1:
                wmk("sync", f"p4_fin_{tglob - 1}", "dv")
            c.dma("sync", t32c[:], acc_d.ap()[tglob], sem="ldb")
            ldacc = c.n["ldb"]
            c.wait("scalar", "ldb", ldacc)
            if tglob == 0:
                c.wait("scalar", "st0", mk["p3_done"]["st0"])
                c.wait("scalar", "st1", mk["p3_done"]["st1"])
            else:
                wmk("scalar", f"p4_mulg_{tglob - 1}", "dv")
            c.s("activation", gbf[:], t32c[:], AF.Silu, sem="ac")
            mark(f"p4_g_{tglob}")
            if tglob >= 1:
                wmk("sync", f"p4_str_{tglob - 1}", "pe")
                wmk("sync", f"p4_out_{tglob - 1}", "st0")
            for oc in range(KC):
                c.dma("sync", t32d[:, oc * 128:(oc + 1) * 128],
                      sT_sp.ap()[oc, :, tglob * 128:(tglob + 1) * 128],
                      sem="lda")
            ld_s = c.n["lda"]
            c.wait("tensor", "lda", ld_s)
            for g in range(2):
                b = 4 + g
                bg_wait(b)
                for f4 in range(4):
                    oc = g * 4 + f4
                    c.tr(psum[:, b * 512 + f4 * 128:b * 512 + f4 * 128 + 128],
                         t32d[:, oc * 128:(oc + 1) * 128], eye32[:, :],
                         sem="pe" if f4 == 3 else None)
                c.wait("vector", "pe", c.n["pe"])
                if g == 0 and tglob == 0:
                    c.wait("vector", "st0", mk["p3_done"]["st0"])
                    c.wait("vector", "st1", mk["p3_done"]["st1"])
                if g == 0 and tglob >= 1:
                    wmk("vector", f"p4_out_{tglob - 1}", "st0")
                c.v("tensor_copy", t32d[:, g * 512:(g + 1) * 512],
                    bank(b), sem="dv")
                bg_set(b, "dv")
            mark(f"p4_str_{tglob}")
            c.wait("sync", "ac", mk[f"p4_g_{tglob}"]["ac"])
            c.dma("sync", t32c[:],
                  x_batch.ap()[tglob * 128:(tglob + 1) * 128, :], sem="ldb")
            ld_x = c.n["ldb"]
            c.wait("vector", "ldb", ld_x)
            c.v("scalar_tensor_tensor", t32d[:], t32c[:],
                stats[:, 16 + tglob:17 + tglob], t32d[:],
                op0=ALU.mult, op1=ALU.add)
            c.v("tensor_single_scalar", t32d[:], t32d[:],
                stats[:, 32 + tglob:33 + tglob], op=ALU.add)
            c.wait("vector", "ac", mk[f"p4_g_{tglob}"]["ac"])
            c.v("tensor_mul", t32d[:], t32d[:], gbf[:], sem="dv")
            mark(f"p4_mulg_{tglob}")
            c.v("tensor_add", t32d[:], t32d[:], t32c[:], sem="dv")
            mark(f"p4_fin_{tglob}")
            c.wait("sync", "dv", c.n["dv"])
            c.dma("sync", out_d.ap()[tglob * 128:(tglob + 1) * 128, :],
                  t32d[:], sem="st0")
            mark(f"p4_out_{tglob}")
        c.wait("sync", "st0", c.n["st0"])
        c.wait("gpsimd", "st0", c.n["st0"])
        if dbg_d:
            c.wait("sync", "dg", c.n["dg"])

    with nc.Block() as block:
        @block.sync
        def _(eng):
            emit(Ctx(nc, "sync", eng, sems))

        @block.gpsimd
        def _(eng):
            emit(Ctx(nc, "gpsimd", eng, sems))

        @block.tensor
        def _(eng):
            emit(Ctx(nc, "tensor", eng, sems))

        @block.vector
        def _(eng):
            emit(Ctx(nc, "vector", eng, sems))

        @block.scalar
        def _(eng):
            emit(Ctx(nc, "scalar", eng, sems))

    return nc


def make_inputs(inputs):
    import ml_dtypes
    x = np.ascontiguousarray(np.asarray(inputs["L0_attn_inputs_embeds"],
                                        np.float32))
    Wqkv = np.asarray(inputs["L0_attn_Wqkv"], np.float32)
    com = {
        "eye32": np.eye(128, dtype=np.float32),
        "eyebf": np.eye(128, dtype=np.float32).astype(ml_dtypes.bfloat16),
        "wq": np.ascontiguousarray(Wqkv[:, 0:C, :]),
        "wkT": np.ascontiguousarray(Wqkv[:, C:2 * C, :].transpose(0, 2, 1)),
        "wvT": np.ascontiguousarray(Wqkv[:, 2 * C:3 * C, :].transpose(0, 2, 1)),
        "wpT": np.ascontiguousarray(
            np.asarray(inputs["L0_attn_Wproj"], np.float32).transpose(0, 2, 1)),
        "bqkv": np.ascontiguousarray(np.asarray(inputs["L0_attn_bqkv"],
                                                np.float32)),
        "bproj": np.ascontiguousarray(np.asarray(inputs["L0_attn_bproj"],
                                                 np.float32)),
        "gwT": np.ascontiguousarray(np.asarray(inputs["L0_ffn_gate_w"],
                                               np.float32).T),
        "gb": np.asarray(inputs["L0_ffn_gate_b"], np.float32).reshape(1, NE),
        "w0T": np.ascontiguousarray(np.asarray(inputs["L0_ffn_w0"],
                                               np.float32).transpose(0, 2, 1)),
        "w1T": np.ascontiguousarray(np.asarray(inputs["L0_ffn_w1"],
                                               np.float32).transpose(0, 2, 1)),
        "w2T": np.ascontiguousarray(np.asarray(inputs["L0_ffn_w2"],
                                               np.float32).transpose(0, 2, 1)),
        "b0": np.ascontiguousarray(np.asarray(inputs["L0_ffn_b0"], np.float32)),
        "b1": np.ascontiguousarray(np.asarray(inputs["L0_ffn_b1"], np.float32)),
        "b2": np.ascontiguousarray(np.asarray(inputs["L0_ffn_b2"], np.float32)),
    }
    in_maps = []
    for core in range(NCORE):
        bi, si = core // 4, core % 4
        xb = x[bi]
        own = xb[si * TOK:(si + 1) * TOK]
        rest = np.concatenate([xb[:si * TOK], xb[(si + 1) * TOK:]], axis=0)
        m = dict(com)
        m["x_batch"] = np.ascontiguousarray(np.concatenate([own, rest], axis=0))
        in_maps.append(m)
    return in_maps


def kernel(**inputs):
    in_maps = make_inputs(inputs)
    nc = build_nc()
    res = run_bass_kernel_spmd(nc, in_maps, core_ids=list(range(NCORE)))
    out = np.empty((B, T, C), np.float32)
    for core in range(NCORE):
        bi, si = core // 4, core % 4
        out[bi, si * TOK:(si + 1) * TOK, :] = res.results[core]["out"]
    return out
